# version 10
# baseline (speedup 1.0000x reference)
from contextlib import ExitStack

import numpy as np
import concourse.bass as bass
import concourse.mybir as mybir
from concourse.bass_utils import run_bass_kernel_spmd


ENGINES = ("pe", "act", "dve", "pool", "sp")


class Buf:
    __slots__ = ("name", "last_write", "readers")

    def __init__(self, name):
        self.name = name
        self.last_write = None
        self.readers = []


class Sched:
    def __init__(self, nc: bass.Bass, es: ExitStack):
        self.nc = nc
        self.es = es
        self.sem = {e: es.enter_context(nc.semaphore(f"sem_{e}")) for e in ENGINES}
        self.cnt = {e: 0 for e in ENGINES}
        self.dsem = {}
        self.dcnt = {}
        self.watermark = {e: {} for e in ENGINES}
        self.items = []
        self._bufs = {}

    def buf(self, name):
        b = self._bufs.get(name)
        if b is None:
            b = self._bufs[name] = Buf(name)
        return b

    def _stream_sem(self, stream):
        s = self.dsem.get(stream)
        if s is None:
            s = self.dsem[stream] = self.es.enter_context(
                self.nc.semaphore(f"dsem_{stream}"))
            self.dcnt[stream] = 0
        return s

    def _collect_waits(self, queue, reads, writes):
        need = {}

        def add(tok):
            if tok is None:
                return
            kind, src, v = tok
            if kind == "d":
                v = self.dcnt[src]
            k = (kind, src)
            if need.get(k, 0) < v:
                need[k] = v

        for b in reads:
            add(b.last_write)
        for b in writes:
            add(b.last_write)
            for t in b.readers:
                add(t)
        waits = []
        wm = self.watermark[queue]
        for k in sorted(need, key=str):
            v = need[k]
            if wm.get(k, 0) >= v:
                continue
            wm[k] = v
            kind, src = k
            sem = self.sem[src] if kind == "c" else self.dsem[src]
            waits.append((sem, v))
        return waits

    def _record(self, tok, reads, writes):
        for b in reads:
            b.readers.append(tok)
        for b in writes:
            b.last_write = tok
            b.readers = []

    def op(self, queue, emit_fn, reads=(), writes=()):
        waits = self._collect_waits(queue, reads, writes)
        self.cnt[queue] += 1
        tok = ("c", queue, self.cnt[queue])
        self.items.append((queue, emit_fn, waits, (self.sem[queue], 1)))
        self._record(tok, reads, writes)
        return tok

    def dma(self, queue, stream, emit_fn, reads=(), writes=()):
        sem = self._stream_sem(stream)
        waits = self._collect_waits(queue, reads, writes)
        self.dcnt[stream] += 16
        tok = ("d", stream, self.dcnt[stream])
        self.items.append((queue, emit_fn, waits, (sem, 16)))
        self._record(tok, reads, writes)
        return tok

    def cc(self, stream, emit_fn, reads=(), writes=(), inc=1):
        sem = self._stream_sem(stream)
        waits = self._collect_waits("pool", reads, writes)
        self.dcnt[stream] += inc
        tok = ("d", stream, self.dcnt[stream])
        self.items.append(("pool", emit_fn, waits, (sem, inc)))
        self._record(tok, reads, writes)
        return tok

    def raw(self, queue, emit_fn):
        self.items.append((queue, emit_fn, [], None))

    def barrier(self):
        targets = []
        for e in ENGINES:
            if self.cnt[e]:
                targets.append(("c", e, self.cnt[e]))
        for stream, c in self.dcnt.items():
            if c:
                targets.append(("d", stream, c))
        for e in ENGINES:
            waits = []
            wm = self.watermark[e]
            for kind, src, v in targets:
                k = (kind, src)
                if kind == "c" and src == e:
                    continue
                if wm.get(k, 0) >= v:
                    continue
                wm[k] = v
                sem = self.sem[src] if kind == "c" else self.dsem[src]
                waits.append((sem, v))

            def emit(h, waits=waits):
                for sem, v in waits:
                    h.wait_ge(sem, v)
                return None

            if waits:
                self.items.append((e, emit, [], None))

    def wait_all_done(self, queue):
        targets = []
        for e in ENGINES:
            if self.cnt[e]:
                targets.append((self.sem[e], self.cnt[e]))
        for stream, c in self.dcnt.items():
            if c:
                targets.append((self.dsem[stream], c))

        def emit(h, targets=targets):
            for sem, v in targets:
                h.wait_ge(sem, v)
            return None

        self.items.append((queue, emit, [], None))

    def build(self, block):
        per_engine = {e: [] for e in ENGINES}
        for it in self.items:
            per_engine[it[0]].append(it)

        def mk_section(items):
            def section(h):
                for _, emit_fn, waits, inc in items:
                    for sem, v in waits:
                        h.wait_ge(sem, v)
                    inst = emit_fn(h)
                    if inc is not None:
                        sem, amt = inc
                        assert inst is not None, "emit_fn must return its instruction"
                        inst.then_inc(sem, amt)
            return section

        if per_engine["pe"]:
            block.tensor(mk_section(per_engine["pe"]))
        if per_engine["act"]:
            block.scalar(mk_section(per_engine["act"]))
        if per_engine["dve"]:
            block.vector(mk_section(per_engine["dve"]))
        if per_engine["pool"]:
            block.gpsimd(mk_section(per_engine["pool"]))
        if per_engine["sp"]:
            block.sync(mk_section(per_engine["sp"]))


F32 = mybir.dt.float32
BF16 = mybir.dt.bfloat16
AF = mybir.ActivationFunctionType
ALU = mybir.AluOpType

D = 2048
DI = 4096
DIS = 1024
S = 16
T = 8192
NR = 4
TS = 2048
BLK = 512
NT = NR * BLK
TSC = 1024
HT = 512
MAGIC = 1.5 * 2 ** 23
G4 = [[0, 1, 2, 3], [4, 5, 6, 7]]
G8 = [[0, 1, 2, 3, 4, 5, 6, 7]]

C_ONE = 0
C_EPS5 = 1
C_EPS6 = 2
C_EPS8 = 3
C_NCONST = 4

PHASES = "WAB"
DEV_DBG = False


def build_program():
    nc = bass.Bass(target_bir_lowering=False, debug=False)

    I = {}

    def inp(name, shape, dt=F32):
        I[name] = nc.dram_tensor(name, shape, dt, kind="ExternalInput").ap()
        return I[name]

    x_own = inp("x_own", [NT, D])
    winT = inp("winT", [D, 2 * DIS])
    woutT = inp("woutT", [DIS, D])
    xprojT = inp("xprojT", [DIS, 33])
    conv_w = inp("conv_w", [128, 8, 4])
    conv_b = inp("conv_b", [128, 8])
    dt_w = inp("dt_w", [128, 8])
    dt_b = inp("dt_b", [128, 8])
    a_log = inp("a_log", [128, 8, S])
    d_prm = inp("d_prm", [128, 8])
    anorm_w = inp("anorm_w", [128, 16])
    fnorm_bc = inp("fnorm_bc", [128, D])
    fgam = inp("fgam", [128, 16])
    wupT_sl = inp("wupT_sl", [D, 1024])
    wdnT_sl = inp("wdnT_sl", [4 * D, 256])
    ident_h = inp("ident_h", [128, 128], BF16)
    ident_f = inp("ident_f", [128, 128])
    ones_row = inp("ones_row", [1, 128])
    ones_col = inp("ones_col", [128, 1])
    sel_bc = inp("sel_bc", [32, 32, 128], BF16)
    sel_dt = inp("sel_dt", [33, 128])
    consts = inp("consts", [128, C_NCONST])

    out_own = nc.dram_tensor("out_own", [NT, D], F32, kind="ExternalOutput").ap()
    dbg = {}
    if DEV_DBG:
        def dbgout(name, shape, dt=F32):
            dbg[name] = nc.dram_tensor(name, shape, dt, kind="ExternalOutput").ap()
        dbgout("dbg_xn", [D, BLK], BF16)
        dbgout("dbg_ssm", [33, TS])
        dbgout("dbg_y", [DIS, TS], BF16)
        dbgout("dbg_xc", [8, 128, TS], BF16)
        dbgout("dbg_sz", [8, 128, TS], BF16)
        dbgout("dbg_ssp", [33, TS])
        dbgout("dbg_x1", [BLK, D])
        dbgout("dbg_rs", [D, BLK])
        dbgout("dbg_xq", [D, BLK], BF16)

    dr = {}

    def scratch(name, shape, dt):
        dr[name] = nc.dram_tensor(name, shape, dt).ap()
        return dr[name]

    for j in range(NR):
        scratch(f"agx_src{j}", [D, BLK], BF16)
        scratch(f"agx_dst{j}", [4 * D, BLK], BF16)
        scratch(f"ars_src{j}", [33, TS], F32)
        scratch(f"ars_dst{j}", [33, TS], F32)
        scratch(f"rs_src{j}", [4 * D, BLK], F32)
        scratch(f"rs_dst{j}", [D, BLK], F32)
        scratch(f"x1_sp{j}", [BLK, D], F32)
        scratch(f"ln2_sp{j}", [BLK, D], F32)
        scratch(f"xqT_sp{j}", [D, BLK], BF16)
        scratch(f"sx_sp{j}", [1, BLK], F32)
    scratch("sp_dt", [8, 128, T], BF16)
    scratch("sp_xc", [8, 128, T], BF16)
    scratch("sp_sz", [8, 128, T], BF16)
    scratch("win_h", [D, 2 * DIS], BF16)
    scratch("wout_h", [DIS, D], BF16)
    scratch("wqu_src", [D, 1024], BF16)
    scratch("wqu_dst", [8 * D, 1024], BF16)
    scratch("wqd_src", [4 * D, 256], BF16)
    scratch("wqd_dst", [32 * D, 256], BF16)
    scratch("sc_src", [1, 2], F32)
    scratch("sc_dst", [1, 2], F32)

    with ExitStack() as es:
        def sb(name, shape, dt=F32):
            return es.enter_context(nc.sbuf_tensor(name, shape, dt)).ap()

        def psum(name, shape, dt=F32):
            return es.enter_context(nc.psum_tensor(name, shape, dt)).ap()

        sb_consts = sb("sb_consts", [128, C_NCONST])
        sb_ident_h = sb("sb_ident_h", [128, 128], BF16)
        sb_ident_f = sb("sb_ident_f", [128, 128])
        sb_ones = sb("sb_ones", [1, 128])
        sb_onesc = sb("sb_onesc", [128, 1])
        sb_sel = sb("sb_sel", [32, 32, 128], BF16)
        sb_seldt = sb("sb_seldt", [33, 128])
        sb_convw = sb("sb_convw", [128, 8, 4])
        sb_convb = sb("sb_convb", [128, 8])
        sb_dtw = sb("sb_dtw", [128, 8])
        sb_dtb = sb("sb_dtb", [128, 8])
        sb_A = sb("sb_A", [128, 8, S])
        sb_D = sb("sb_D", [128, 8])
        sb_anw = sb("sb_anw", [128, 16])
        sb_fg = sb("sb_fg", [128, 16])
        sb_carry = sb("sb_carry", [128, 8, S])
        sb_scales = sb("sb_scales", [1, 4])
        sb_rcol = sb("sb_rcol", [128, 4])
        sb_xpj = sb("sb_xpj", [128, 8 * 33], BF16)
        psA = psum("psA", [128, 512])
        psB = psum("psB", [128, 512])
        psC = psum("psC", [128, 512])
        psD = psum("psD", [128, 512])
        psE = psum("psE", [128, 512])
        psF = psum("psF", [128, 512])
        psG = psum("psG", [128, 1024], BF16)
        psH = psum("psH", [128, 1024], BF16)

        block = es.enter_context(nc.Block())
        s = Sched(nc, es)
        B = s.buf

        es_wa = ExitStack()
        def sbwa(name, shape, dt=F32):
            return es_wa.enter_context(nc.sbuf_tensor(name, shape, dt)).ap()
        w_ld = sbwa("w_ld", [128, 2, 2048])
        w_cast = sbwa("w_cast", [128, 2, 2048], BF16)
        a_x = sbwa("a_x", [128, D])
        a_scr = sbwa("a_scr", [128, D], BF16)
        a_xn = sbwa("a_xn", [128, D], BF16)
        a_stat = sbwa("a_stat", [128, 12])
        a_feat = sbwa("a_feat", [128, 16, 128], BF16)

        def ld(queue, stream, dst_ap, src_ap, wbuf, rbufs=()):
            return s.dma(queue, stream, lambda h: h.dma_start(out=dst_ap, in_=src_ap),
                         reads=list(rbufs), writes=[wbuf])

        def st(queue, stream, dst_ap, src_ap, rbuf, wbufs=()):
            return s.dma(queue, stream, lambda h: h.dma_start(out=dst_ap, in_=src_ap),
                         reads=[rbuf], writes=list(wbufs))

        def mmgroup(psum_ap, pairs, pbuf, rbufs):
            def emit(h):
                inst = None
                n = len(pairs)
                for i, (lt, rh) in enumerate(pairs):
                    inst = h.matmul(psum_ap, lt, rh, start=(i == 0), stop=(i == n - 1))
                return inst
            return s.op("pe", emit, reads=list(rbufs), writes=[pbuf])

        ld("sp", "ldc", sb_consts, consts, B("consts"))
        ld("sp", "ldc", sb_ident_h, ident_h, B("identh"))
        ld("sp", "ldc", sb_ident_f, ident_f, B("identf"))
        ld("sp", "ldc", sb_ones, ones_row, B("ones"))
        ld("sp", "ldc", sb_onesc, ones_col, B("onesc"))
        ld("sp", "ldc", sb_sel, sel_bc, B("sel"))
        ld("sp", "ldc", sb_seldt, sel_dt, B("seldt"))
        ld("sp", "ldc", sb_convw, conv_w, B("convw"))
        ld("sp", "ldc", sb_convb, conv_b, B("convb"))
        ld("sp", "ldc", sb_dtw, dt_w, B("dtw"))
        ld("sp", "ldc", sb_dtb, dt_b, B("dtb"))
        ld("sp", "ldc", sb_A, a_log, B("A"))
        ld("sp", "ldc", sb_D, d_prm, B("D"))
        ld("sp", "ldc", sb_anw, anorm_w, B("anw"))
        ld("sp", "ldc", sb_fg, fgam, B("fg"))
        ld("sp", "ldc", w_ld[:, 0, 0:8 * 33].rearrange("p (k m) -> p k m", k=8),
           xprojT.rearrange("(k p) m -> p k m", p=128), B("wld0"))
        s.op("act", lambda h: h.activation(sb_A, sb_A, AF.Exp), reads=[B("A")], writes=[B("A")])
        s.op("dve", lambda h: h.tensor_scalar(sb_A, sb_A, -1.0, None, ALU.mult),
             reads=[B("A")], writes=[B("A")])
        s.op("dve", lambda h: h.tensor_copy(sb_xpj, w_ld[:, 0, 0:8 * 33]),
             reads=[B("wld0")], writes=[B("xpj")])

        x_own_v = x_own.rearrange("(j tt p) d -> p j tt d", p=128, j=NR)
        for j in range(NR):
            agx_src_v = dr[f"agx_src{j}"].rearrange("(kc p) t -> p kc t", p=128)
            for tt in range(4):
                xb = B("ax")
                ld("sp", "ax", a_x, x_own_v[:, j, tt, :], xb)
                s.op("act", lambda h: h.activation(
                    a_scr, a_x, AF.Copy, accum_out=a_stat[:, 0:1]),
                    reads=[xb], writes=[B("ascr"), B("ast")])
                s.op("act", lambda h: h.activation(
                    a_scr, a_x, AF.Square, accum_out=a_stat[:, 1:2]),
                    reads=[xb], writes=[B("ascr"), B("ast")])
                s.op("dve", lambda h: h.tensor_scalar(a_stat[:, 2:3], a_stat[:, 0:1],
                                                      1.0 / D, None, ALU.mult),
                     reads=[B("ast")], writes=[B("ast")])
                s.op("dve", lambda h: h.tensor_scalar(a_stat[:, 3:4], a_stat[:, 1:2],
                                                      1.0 / D, None, ALU.mult),
                     reads=[B("ast")], writes=[B("ast")])
                s.op("dve", lambda h: h.tensor_scalar(a_stat[:, 4:5], a_stat[:, 2:3],
                                                      a_stat[:, 2:3], None, ALU.mult),
                     reads=[B("ast")], writes=[B("ast")])
                s.op("dve", lambda h: h.tensor_tensor(a_stat[:, 3:4], a_stat[:, 3:4],
                                                      a_stat[:, 4:5], ALU.subtract),
                     reads=[B("ast")], writes=[B("ast")])
                s.op("act", lambda h: h.activation(a_stat[:, 5:6], a_stat[:, 3:4], AF.Ln,
                                                   bias=sb_consts[:, C_EPS5:C_EPS5 + 1]),
                     reads=[B("ast"), B("consts")], writes=[B("ast")])
                s.op("act", lambda h: h.activation(a_stat[:, 5:6], a_stat[:, 5:6], AF.Exp,
                                                   scale=-0.5),
                     reads=[B("ast")], writes=[B("ast")])
                s.op("act", lambda h: h.activation(a_stat[:, 6:7], a_stat[:, 5:6], AF.Square),
                     reads=[B("ast")], writes=[B("ast")])
                s.op("dve", lambda h: h.tensor_tensor(a_stat[:, 6:7], a_stat[:, 6:7],
                                                      a_stat[:, 3:4], ALU.mult),
                     reads=[B("ast")], writes=[B("ast")])
                s.op("act", lambda h: h.activation(a_stat[:, 6:7], a_stat[:, 6:7], AF.Ln,
                                                   bias=sb_consts[:, C_EPS5:C_EPS5 + 1]),
                     reads=[B("ast"), B("consts")], writes=[B("ast")])
                s.op("act", lambda h: h.activation(a_stat[:, 6:7], a_stat[:, 6:7], AF.Exp,
                                                   scale=-0.5),
                     reads=[B("ast")], writes=[B("ast")])
                s.op("dve", lambda h: h.tensor_tensor(a_stat[:, 7:8], a_stat[:, 5:6],
                                                      a_stat[:, 6:7], ALU.mult),
                     reads=[B("ast")], writes=[B("ast")])
                s.op("dve", lambda h: h.tensor_scalar(a_stat[:, 8:9], a_stat[:, 2:3],
                                                      a_stat[:, 7:8], -1.0, ALU.mult, ALU.mult),
                     reads=[B("ast")], writes=[B("ast")])
                s.op("act", lambda h: h.activation(
                    a_xn, a_x, AF.Identity,
                    scale=a_stat[:, 7:8], bias=a_stat[:, 8:9]),
                    reads=[xb, B("ast")], writes=[B("axn")])
                for kc in range(16):
                    pg = (psG, psH)[kc % 2]
                    pgb = B(f"psT{kc % 2}")
                    s.op("pe", lambda h, kc=kc, pg=pg: h.transpose(
                        pg[:, 0:128], a_xn[:, kc * 128:(kc + 1) * 128], sb_ident_h),
                        reads=[B("axn"), B("identh")], writes=[pgb])
                    s.op("dve", lambda h, kc=kc, pg=pg: h.tensor_scalar(
                        a_feat[:, kc, :], pg[:, 0:128], sb_anw[:, kc:kc + 1], None,
                        ALU.mult), reads=[pgb, B("anw")], writes=[B("afeat")])
                st("act", f"agx{j}", agx_src_v[:, :, tt * 128:(tt + 1) * 128],
                   a_feat, B("afeat"), wbufs=[B(f"agxs{j}")])
            s.cc(f"ccx{j}", lambda h, j=j: h.collective_compute(
                "AllGather", ALU.bypass, replica_groups=G4,
                ins=[dr[f"agx_src{j}"]], outs=[dr[f"agx_dst{j}"]]),
                reads=[B(f"agxs{j}")], writes=[B(f"agxd{j}")])
        if DEV_DBG:
            s.dma("sp", "dbg", lambda h: h.dma_start(
                out=dbg["dbg_xn"].rearrange("(kc p) t -> p kc t", p=128),
                in_=dr["agx_src0"].rearrange("(kc p) t -> p kc t", p=128)),
                reads=[B("agxs0")], writes=[B("dbgxn")])


        def cast_spill(src_ap, dst_ap, rows, cols, stream, dstbuf):
            nro = rows // 128
            src_v = src_ap.rearrange("(r p) m -> p r m", p=128)
            dst_v = dst_ap.rearrange("(r p) m -> p r m", p=128)
            for r in range(nro):
                par = r % 2
                wb, cb = B(f"wld{par}"), B(f"wcast{par}")
                ld("sp", f"wld{par}", w_ld[:, par, 0:cols], src_v[:, r, :], wb)
                s.op("dve", lambda h, par=par, cols=cols: h.tensor_copy(
                    w_cast[:, par, 0:cols], w_ld[:, par, 0:cols]), reads=[wb], writes=[cb])
                st("act", stream, dst_v[:, r, :], w_cast[:, par, 0:cols], cb, wbufs=[dstbuf])

        cast_spill(winT, dr["win_h"], D, 2 * DIS, "wspill", B("win_spill"))
        cast_spill(woutT, dr["wout_h"], DIS, D, "wspill", B("wout_spill"))

        def emit_quant():
            q_ld = es_b1.enter_context(nc.sbuf_tensor("q_ld", [128, 2, 512], F32)).ap()
            q_ct = es_b1.enter_context(nc.sbuf_tensor("q_ct", [128, 2, 512], BF16)).ap()
            q_acc = es_b1.enter_context(nc.sbuf_tensor("q_acc", [128, 1], F32)).ap()
            q_s = es_b1.enter_context(nc.sbuf_tensor("q_s", [128, 2], F32)).ap()
            s.op("dve", lambda h: h.memset(q_s, 0.0), writes=[B("qs")])

            def quant_abs(src_ap, rows, cols, col_idx):
                nro = rows // 128
                src_v = src_ap.rearrange("(r p) m -> p r m", p=128)
                i = 0
                for r in range(nro):
                    for c0 in range(0, cols, 512):
                        ce = min(c0 + 512, cols)
                        par = i % 2
                        i += 1
                        wb = B(f"qld{par}")
                        ld("sp", f"qld{par}", q_ld[:, par, 0:ce - c0], src_v[:, r, c0:ce], wb)
                        s.op("act", lambda h, par=par, n=ce - c0: h.activation(
                            q_ct[:, par, 0:n], q_ld[:, par, 0:n], AF.Abs,
                            accum_out=q_acc), reads=[wb], writes=[B(f"qct{par}"), B("qacc")])
                        s.op("dve", lambda h, col_idx=col_idx: h.tensor_tensor(
                            q_s[:, col_idx:col_idx + 1], q_s[:, col_idx:col_idx + 1],
                            q_acc, ALU.add), reads=[B("qacc"), B("qs")], writes=[B("qs")])

            quant_abs(wupT_sl, D, 1024, 0)
            quant_abs(wdnT_sl, 4 * D, 256, 1)
            s.op("pe", lambda h: h.matmul(psE[:1, 0:2], sb_onesc,
                                          q_s, start=True, stop=True),
                 reads=[B("qs"), B("onesc")], writes=[B("psE")])
            s.op("dve", lambda h: h.tensor_copy(sb_scales[:, 0:2], psE[:1, 0:2]),
                 reads=[B("psE")], writes=[B("scales")])
            st("act", "ccsc", dr["sc_src"], sb_scales[:, 0:2], B("scales"), wbufs=[B("scsrc")])
            s.cc("ccq", lambda h: h.collective_compute(
                "AllReduce", ALU.add, replica_groups=G8,
                ins=[dr["sc_src"]], outs=[dr["sc_dst"]]),
                reads=[B("scsrc")], writes=[B("scdst")])
            ld("sp", "ccsc", sb_scales[:, 0:2], dr["sc_dst"], B("scales"), rbufs=[B("scdst")])
            for ci in range(2):
                s.op("dve", lambda h, ci=ci: h.tensor_scalar(
                    sb_scales[:, ci:ci + 1], sb_scales[:, ci:ci + 1],
                    1.0 / (8192 * 2048), 1e-8, ALU.mult, ALU.max),
                    reads=[B("scales")], writes=[B("scales")])
            s.op("dve", lambda h: h.reciprocal(sb_scales[:, 2:4], sb_scales[:, 0:2]),
                 reads=[B("scales")], writes=[B("scales")])
            s.op("pe", lambda h: h.matmul(psE[:, 0:4], sb_ones, sb_scales[:1, 0:4],
                                          start=True, stop=True),
                 reads=[B("scales"), B("ones")], writes=[B("psE")])
            s.op("dve", lambda h: h.tensor_copy(sb_rcol, psE[:, 0:4]),
                 reads=[B("psE")], writes=[B("rcol")])

            def quant_pass(src_ap, dst_ap, rows, cols, col_idx, dstbuf):
                nro = rows // 128
                src_v = src_ap.rearrange("(r p) m -> p r m", p=128)
                dst_v = dst_ap.rearrange("(r p) m -> p r m", p=128)
                i = 0
                for r in range(nro):
                    for c0 in range(0, cols, 512):
                        ce = min(c0 + 512, cols)
                        par = i % 2
                        i += 1
                        wb, cb = B(f"qld{par}"), B(f"qct{par}")
                        ld("sp", f"qld{par}", q_ld[:, par, 0:ce - c0], src_v[:, r, c0:ce], wb)
                        s.op("dve", lambda h, par=par, n=ce - c0, col_idx=col_idx: h.tensor_scalar(
                            q_ld[:, par, 0:n], q_ld[:, par, 0:n],
                            sb_rcol[:, col_idx:col_idx + 1], None, ALU.mult),
                            reads=[wb, B("rcol")], writes=[wb])
                        s.op("dve", lambda h, par=par, n=ce - c0: h.tensor_scalar(
                            q_ld[:, par, 0:n], q_ld[:, par, 0:n], MAGIC, MAGIC,
                            ALU.add, ALU.subtract), reads=[wb], writes=[wb])
                        s.op("dve", lambda h, par=par, n=ce - c0: h.tensor_scalar(
                            q_ct[:, par, 0:n], q_ld[:, par, 0:n], 1.0, -1.0,
                            ALU.min, ALU.max), reads=[wb], writes=[cb])
                        st("act", "qspill", dst_v[:, r, c0:ce], q_ct[:, par, 0:ce - c0], cb,
                           wbufs=[dstbuf])

            quant_pass(wupT_sl, dr["wqu_src"], D, 1024, 2, B("wqusrc"))
            quant_pass(wdnT_sl, dr["wqd_src"], 4 * D, 256, 3, B("wqdsrc"))
            s.cc("ccq", lambda h: h.collective_compute(
                "AllGather", ALU.bypass, replica_groups=G8,
                ins=[dr["wqu_src"]], outs=[dr["wqu_dst"]]),
                reads=[B("wqusrc")], writes=[B("wqu")])
            s.cc("ccq", lambda h: h.collective_compute(
                "AllGather", ALU.bypass, replica_groups=G8,
                ins=[dr["wqd_src"]], outs=[dr["wqd_dst"]]),
                reads=[B("wqdsrc")], writes=[B("wqd")])


        s.barrier()
        es_wa.close()
        es_b = ExitStack()
        es_b1 = ExitStack()
        def sbb(name, shape, dt=F32):
            return es_b.enter_context(nc.sbuf_tensor(name, shape, dt)).ap()
        def sbb1(name, shape, dt=F32):
            return es_b1.enter_context(nc.sbuf_tensor(name, shape, dt)).ap()
        b_smh = sbb("b_smh", [32, NR, TS], BF16)
        b_ops = sbb("b_ops", [128, 2, 512])
        b_rhs = sbb1("b_rhs", [128, 16, HT], BF16)
        b_w = sbb1("b_w", [128, 2, 16, 128], BF16)
        b_xz = sbb1("b_xz", [128, 8, HT + 3], BF16)
        b_halo = sbb1("b_halo", [128, 8, 3], BF16)
        b_zst = sbb1("b_zst", [128, 1, HT], BF16)
        b_xcm = sbb1("b_xcm", [128, 8, HT], BF16)
        b_ssm = sbb1("b_ssm", [33, TS])
        b_dtrb = sbb1("b_dtrb", [128, HT])
        b_dt = sbb1("b_dt", [128, 2, HT])
        b_dth = sbb1("b_dth", [128, 1, HT], BF16)
        b_sps = sbb1("b_sps", [128, 2, HT])

        win_v = dr["win_h"].rearrange("(kc p) m -> p kc m", p=128)
        wo_v = dr["wout_h"].rearrange("(kc p) m -> p kc m", p=128)
        sp_dt_v = dr["sp_dt"].rearrange("k p t -> p k t")
        sp_xc_v = dr["sp_xc"].rearrange("k p t -> p k t")
        sp_sz_v = dr["sp_sz"].rearrange("k p t -> p k t")

        def prescan(j):
            agx_dst_v = dr[f"agx_dst{j}"].rearrange(
                "(r kc p) t -> p kc r t", p=128, kc=16)
            for half in range(4):
                t0 = half * HT
                rb = B("brhs")
                s.dma("sp", "brhs", lambda h, half=half, v=agx_dst_v: h.dma_start(
                    out=b_rhs, in_=v[:, :, half, :]),
                    reads=[B(f"agxd{j}")], writes=[rb])
                if j == 0 and half == 0:
                    s.op("dve", lambda h: h.memset(b_halo, 0.0), writes=[B("bhalo")])
                s.op("dve", lambda h: h.tensor_copy(b_xz[:, :, 0:3], b_halo),
                     reads=[B("bhalo")], writes=[B("bxz")])
                for m in range(16):
                    par = m % 2
                    wb = B(f"bw{par}")
                    s.dma("sp", f"bw{par}", lambda h, m=m, par=par: h.dma_start(
                        out=b_w[:, par, :, :], in_=win_v[:, :, m * 128:(m + 1) * 128]),
                        reads=[B("win_spill")], writes=[wb])
                    pbank, pb = ((psA, B("psA")), (psB, B("psB")))[par]
                    pairs = [(b_w[:, par, kc, :], b_rhs[:, kc, :]) for kc in range(16)]
                    mmgroup(pbank, pairs, pb, [wb, rb])
                    if m < 8:
                        s.op("act", lambda h, m=m, pbank=pbank: h.activation(
                            b_xz[:, m, 3:3 + HT], pbank, AF.Copy),
                            reads=[pb], writes=[B("bxz")])
                    else:
                        s.op("act", lambda h, pbank=pbank: h.activation(
                            b_zst[:, 0, :], pbank, AF.Silu), reads=[pb], writes=[B("bzst")])
                        st("act", "spsz",
                           sp_sz_v[:, m - 8, j * TS + t0:j * TS + t0 + HT],
                           b_zst[:, 0, :], B("bzst"), wbufs=[B("spsz")])
                for k in range(8):
                    s.op("dve", lambda h, k=k: h.tensor_scalar(
                        b_xcm[:, k, :], b_xz[:, k, 0:HT], sb_convw[:, k, 0:1],
                        sb_convb[:, k:k + 1], ALU.mult, ALU.add),
                        reads=[B("bxz"), B("convw"), B("convb")], writes=[B("bxcm")])
                    for tap in range(1, 4):
                        s.op("dve", lambda h, k=k, tap=tap: h.scalar_tensor_tensor(
                            b_xcm[:, k, :], b_xz[:, k, tap:tap + HT],
                            sb_convw[:, k, tap:tap + 1], b_xcm[:, k, :],
                            ALU.mult, ALU.add),
                            reads=[B("bxz"), B("convw"), B("bxcm")], writes=[B("bxcm")])
                    s.op("act", lambda h, k=k: h.activation(
                        b_xcm[:, k, :], b_xcm[:, k, :], AF.Silu),
                        reads=[B("bxcm")], writes=[B("bxcm")])
                s.op("dve", lambda h: h.tensor_copy(b_halo, b_xz[:, :, HT:HT + 3]),
                     reads=[B("bxz")], writes=[B("bhalo")])
                st("act", "spxc", sp_xc_v[:, :, j * TS + t0:j * TS + t0 + HT],
                   b_xcm, B("bxcm"), wbufs=[B("spxc")])
                pairs = [(sb_xpj[:, kc * 33:(kc + 1) * 33], b_xcm[:, kc, :])
                         for kc in range(8)]
                mmgroup(psC[0:33, :], pairs, B("psC"), [B("xpj"), B("bxcm")])
                s.op("act", lambda h, t0=t0: h.activation(
                    b_ssm[0:33, t0:t0 + HT], psC[0:33, :], AF.Copy),
                    reads=[B("psC")], writes=[B("bssm")])
            st("act", f"ars{j}", dr[f"ars_src{j}"], b_ssm, B("bssm"), wbufs=[B(f"arss{j}")])
            s.cc(f"ccr{j}", lambda h, j=j: h.collective_compute(
                "AllReduce", ALU.add, replica_groups=G4,
                ins=[dr[f"ars_src{j}"]], outs=[dr[f"ars_dst{j}"]]),
                reads=[B(f"arss{j}")], writes=[B(f"arsd{j}")])
            ld("sp", "arsl", b_ssm, dr[f"ars_dst{j}"], B("bssm"), rbufs=[B(f"arsd{j}")])
            s.op("act", lambda h, j=j: h.activation(b_smh[:, j, :], b_ssm[0:32, :],
                                                    AF.Copy),
                 reads=[B("bssm")], writes=[B("bsmh")])
            for half in range(4):
                t0 = half * HT
                pbank, pb = ((psE, B("psE")), (psF, B("psF")))[half % 2]
                s.op("pe", lambda h, t0=t0, pbank=pbank: h.matmul(
                    pbank, sb_seldt, b_ssm[:, t0:t0 + HT], start=True, stop=True),
                    reads=[B("bssm"), B("seldt")], writes=[pb])
                s.op("dve", lambda h, pbank=pbank: h.tensor_copy(b_dtrb, pbank),
                     reads=[pb], writes=[B("bdtrb")])
                for k in range(8):
                    par = k % 2
                    s.op("dve", lambda h, k=k, par=par: h.tensor_scalar(
                        b_dt[:, par, :], b_dtrb, sb_dtw[:, k:k + 1], sb_dtb[:, k:k + 1],
                        ALU.mult, ALU.add),
                        reads=[B("bdtrb"), B("dtw"), B("dtb")], writes=[B(f"bdt{par}")])
                    s.op("act", lambda h, par=par: h.activation(
                        b_sps[:, par, :], b_dt[:, par, :], AF.Abs),
                        reads=[B(f"bdt{par}")], writes=[B(f"bsps{par}")])
                    s.op("act", lambda h, par=par: h.activation(
                        b_sps[:, par, :], b_sps[:, par, :], AF.Exp, scale=-1.0),
                        reads=[B(f"bsps{par}")], writes=[B(f"bsps{par}")])
                    s.op("act", lambda h, par=par: h.activation(
                        b_sps[:, par, :], b_sps[:, par, :], AF.Ln,
                        bias=sb_consts[:, C_ONE:C_ONE + 1]),
                        reads=[B(f"bsps{par}"), B("consts")], writes=[B(f"bsps{par}")])
                    s.op("dve", lambda h, par=par: h.scalar_tensor_tensor(
                        b_dt[:, par, :], b_dt[:, par, :], 0.0, b_sps[:, par, :],
                        ALU.max, ALU.add),
                        reads=[B(f"bdt{par}"), B(f"bsps{par}")], writes=[B(f"bdt{par}")])
                    s.op("act", lambda h, par=par: h.activation(
                        b_dth[:, 0, :], b_dt[:, par, :], AF.Copy),
                        reads=[B(f"bdt{par}")], writes=[B("bdth")])
                    st("act", "spdt",
                       sp_dt_v[:, k, j * TS + t0:j * TS + t0 + HT],
                       b_dth[:, 0, :], B("bdth"), wbufs=[B("spdt")])

        SCAN = {}
        def alloc_scan():
            SCAN["sc_dt"] = sbb("sc_dt", [128, 2, 8, TSC], BF16)
            SCAN["sc_xcz"] = sbb("sc_xcz", [128, 2, 8, TSC], BF16)
            SCAN["sc_dtu"] = sbb("sc_dtu", [128, 8, TSC], BF16)
            SCAN["sc_bc"] = sbb("sc_bc", [128, 2, 2, TSC], BF16)
            SCAN["sc_da"] = sbb("sc_da", [128, 2, TSC], BF16)
            SCAN["sc_bg"] = sbb("sc_bg", [128, 8, TSC], BF16)
            SCAN["sc_g"] = sbb("sc_g", [128, 8, TSC], BF16)
            SCAN["sc_h"] = sbb("sc_h", [128, 2, 8, TSC], BF16)
            SCAN["sc_y"] = sbb("sc_y", [128, 8, TSC], BF16)
            SCAN["ow_w"] = sbb("ow_w", [128, 2, 8, 128], BF16)

        def scanphase(j):
            sc_dt = SCAN["sc_dt"]; sc_xcz = SCAN["sc_xcz"]; sc_dtu = SCAN["sc_dtu"]
            sc_bc = SCAN["sc_bc"]; sc_da = SCAN["sc_da"]; sc_bg = SCAN["sc_bg"]
            sc_g = SCAN["sc_g"]; sc_h = SCAN["sc_h"]; sc_y = SCAN["sc_y"]
            ow_w = SCAN["ow_w"]
            rs_v = dr[f"rs_src{j}"].rearrange("(blk kc p) t -> p blk kc t",
                                              p=128, blk=4)
            for c in range(TS // TSC):
                tc = j * TS + c * TSC
                tcl = c * TSC
                cp = (j * (TS // TSC) + c) % 2
                dtb = B(f"scdt{cp}")
                xzb = B(f"scxcz{cp}")
                ld("sp", f"scdt{cp}", sc_dt[:, cp], sp_dt_v[:, :, tc:tc + TSC], dtb,
                   rbufs=[B("spdt")])
                ld("sp", f"scxc{cp}", sc_xcz[:, cp], sp_xc_v[:, :, tc:tc + TSC], xzb,
                   rbufs=[B("spxc")])
                s.op("dve", lambda h, cp=cp: h.tensor_tensor(
                    sc_dtu, sc_dt[:, cp], sc_xcz[:, cp], ALU.mult),
                    reads=[dtb, xzb], writes=[B("scdtu")])
                dbc = sb_D.unsqueeze(2).broadcast_to([128, 8, TSC])
                s.op("pool", lambda h, cp=cp, dbc=dbc: h.tensor_tensor(
                    sc_y, sc_xcz[:, cp], dbc, ALU.mult),
                    reads=[xzb, B("D")], writes=[B("scy")])
                ld("sp", f"scxc{cp}", sc_xcz[:, cp], sp_sz_v[:, :, tc:tc + TSC], xzb,
                   rbufs=[B("spsz")])
                for si in range(S):
                    par = si % 2
                    for row, base in ((0, si), (1, 16 + si)):
                        pbank, pb = ((psE, B("psE")), (psF, B("psF")))[row]
                        for q in range(2):
                            s.op("pe", lambda h, base=base, j=j, tcl=tcl, q=q, pbank=pbank:
                                 h.matmul(pbank, sb_sel[:, base, :],
                                          b_smh[:, j, tcl + q * 512:tcl + (q + 1) * 512],
                                          start=True, stop=True),
                                 reads=[B("bsmh"), B("sel")], writes=[pb])
                            s.op("act", lambda h, row=row, par=par, q=q, pbank=pbank:
                                 h.activation(sc_bc[:, par, row, q * 512:(q + 1) * 512],
                                              pbank, AF.Copy),
                                 reads=[pb], writes=[B(f"scbc{par}_{row}")])
                    bb = sc_bc[:, par, 0, :].unsqueeze(1).broadcast_to([128, 8, TSC])
                    cb = sc_bc[:, par, 1, :].unsqueeze(1).broadcast_to([128, 8, TSC])
                    s.op("dve", lambda h, bb=bb: h.tensor_tensor(sc_bg, sc_dtu, bb,
                                                                 ALU.mult),
                         reads=[B("scdtu"), B(f"scbc{par}_0")], writes=[B("scbg")])
                    for k in range(8):
                        kp = k % 2
                        s.op("act", lambda h, k=k, si=si, kp=kp, cp=cp: h.activation(
                            sc_da[:, kp, :], sc_dt[:, cp, k, :], AF.Exp,
                            scale=sb_A[:, k, si:si + 1]),
                            reads=[dtb, B("A")], writes=[B(f"scda{kp}")])
                        first = (j == 0 and c == 0)
                        carry_b = B(f"carry{k}_{si}")
                        s.op("dve", lambda h, k=k, si=si, kp=kp, par=par, first=first:
                             h.tensor_tensor_scan(
                                 sc_h[:, par, k, :], sc_da[:, kp, :], sc_bg[:, k, :],
                                 0.0 if first else sb_carry[:, k, si:si + 1],
                                 ALU.mult, ALU.add),
                             reads=[B(f"scda{kp}"), B("scbg")] +
                                   ([] if first else [carry_b]),
                             writes=[B(f"sch{par}")])
                        s.op("pool", lambda h, k=k, si=si, par=par: h.tensor_copy(
                            sb_carry[:, k, si:si + 1], sc_h[:, par, k, TSC - 1:TSC]),
                            reads=[B(f"sch{par}")], writes=[carry_b])
                    s.op("dve", lambda h, cb=cb, par=par: h.tensor_tensor(
                        sc_g, sc_h[:, par], cb, ALU.mult),
                        reads=[B(f"sch{par}"), B(f"scbc{par}_1")], writes=[B("scg")])
                    s.op("pool", lambda h: h.tensor_tensor(sc_y, sc_y, sc_g, ALU.add),
                         reads=[B("scg"), B("scy")], writes=[B("scy")])
                s.op("pool", lambda h, cp=cp: h.tensor_tensor(
                    sc_y, sc_y, sc_xcz[:, cp], ALU.mult),
                    reads=[B("scy"), xzb], writes=[B("scy")])
                for bcol in range(2):
                    blk = c * 2 + bcol
                    for m in range(16):
                        par = m % 2
                        owb = B(f"oww{par}")
                        s.dma("sp", f"oww{par}", lambda h, m=m, par=par: h.dma_start(
                            out=ow_w[:, par, :, :], in_=wo_v[:, :, m * 128:(m + 1) * 128]),
                            reads=[B("wout_spill")], writes=[owb])
                        pbank, pb = ((psA, B("psA")), (psB, B("psB")))[par]
                        pairs = [(ow_w[:, par, kc, :],
                                  sc_y[:, kc, bcol * 512:(bcol + 1) * 512])
                                 for kc in range(8)]
                        mmgroup(pbank, pairs, pb, [owb, B("scy")])
                        s.op("act", lambda h, par=par, pbank=pbank: h.activation(
                            b_ops[:, par, :], pbank, AF.Copy),
                            reads=[pb], writes=[B(f"opst{par}")])
                        s.dma("act", f"rs{j}", lambda h, m=m, blk=blk, par=par,
                              v=rs_v: h.dma_start(
                            out=v[:, blk, m, :], in_=b_ops[:, par, :]),
                            reads=[B(f"opst{par}")], writes=[B(f"rss{j}")])
            s.cc(f"ccy{j}", lambda h, j=j: h.collective_compute(
                "ReduceScatter", ALU.add, replica_groups=G4,
                ins=[dr[f"rs_src{j}"]], outs=[dr[f"rs_dst{j}"]]),
                reads=[B(f"rss{j}")], writes=[B(f"rsd{j}")])

        if "B" in PHASES:
            prescan(0)
            prescan(1)
            prescan(2)
            prescan(3)
            emit_quant()
            s.barrier()
            es_b1.close()
            alloc_scan()
            scanphase(0)
            scanphase(1)
            scanphase(2)
            scanphase(3)

        s.barrier()
        es_b.close()
        es_c = ExitStack()
        def sbc(name, shape, dt=F32):
            return es_c.enter_context(nc.sbuf_tensor(name, shape, dt)).ap()
        c_in = sbc("c_in", [128, 16, BLK])
        c_tok = sbc("c_tok", [128, 4, D])
        c_x = sbc("c_x", [128, D])
        c_ln2 = sbc("c_ln2", [128, D])
        c_xn2 = sbc("c_xn2", [128, D])
        c_scr = sbc("c_scr", [128, D], BF16)
        c_xq = sbc("c_xq", [128, D], BF16)
        c_xqT = sbc("c_xqT", [128, 16, 128], BF16)
        c_stat = sbc("c_stat", [128, 12])
        c_sxr = sbc("c_sxr", [1, BLK])
        sb_fnw = sbc("sb_fnw", [128, D])
        ld("sp", "ldfnw", sb_fnw, fnorm_bc, B("fnw"))

        def ln_stats(tile_ap, tilebuf, eps_col):
            s.op("act", lambda h: h.activation(c_scr, tile_ap, AF.Copy,
                                               accum_out=c_stat[:, 0:1]),
                 reads=[tilebuf], writes=[B("cscr"), B("cst")])
            s.op("act", lambda h: h.activation(c_scr, tile_ap, AF.Square,
                                               accum_out=c_stat[:, 1:2]),
                 reads=[tilebuf], writes=[B("cscr"), B("cst")])
            s.op("dve", lambda h: h.tensor_scalar(c_stat[:, 2:3], c_stat[:, 0:1],
                                                  1.0 / D, None, ALU.mult),
                 reads=[B("cst")], writes=[B("cst")])
            s.op("dve", lambda h: h.tensor_scalar(c_stat[:, 3:4], c_stat[:, 1:2],
                                                  1.0 / D, None, ALU.mult),
                 reads=[B("cst")], writes=[B("cst")])
            s.op("dve", lambda h: h.tensor_scalar(c_stat[:, 4:5], c_stat[:, 2:3],
                                                  c_stat[:, 2:3], None, ALU.mult),
                 reads=[B("cst")], writes=[B("cst")])
            s.op("dve", lambda h: h.tensor_tensor(c_stat[:, 3:4], c_stat[:, 3:4],
                                                  c_stat[:, 4:5], ALU.subtract),
                 reads=[B("cst")], writes=[B("cst")])
            s.op("act", lambda h, eps_col=eps_col: h.activation(
                c_stat[:, 5:6], c_stat[:, 3:4], AF.Ln,
                bias=sb_consts[:, eps_col:eps_col + 1]),
                 reads=[B("cst"), B("consts")], writes=[B("cst")])
            s.op("act", lambda h: h.activation(c_stat[:, 5:6], c_stat[:, 5:6], AF.Exp,
                                               scale=-0.5),
                 reads=[B("cst")], writes=[B("cst")])
            s.op("dve", lambda h: h.tensor_scalar(c_stat[:, 8:9], c_stat[:, 2:3],
                                                  c_stat[:, 5:6], -1.0, ALU.mult, ALU.mult),
                 reads=[B("cst")], writes=[B("cst")])

        for j in range(NR):
            rsd_v = dr[f"rs_dst{j}"].rearrange("(kc p) t -> p kc t", p=128)
            ld("sp", "cin", c_in, rsd_v, B("cin"), rbufs=[B(f"rsd{j}")])
            for kc in range(16):
                for tt in range(4):
                    par = (kc * 4 + tt) % 2
                    pbank, pb = ((psE, B("psE")), (psF, B("psF")))[par]
                    s.op("pe", lambda h, kc=kc, tt=tt, pbank=pbank: h.transpose(
                        pbank[:, 0:128], c_in[:, kc, tt * 128:(tt + 1) * 128], sb_ident_f),
                        reads=[B("cin"), B("identf")], writes=[pb])
                    s.op("act", lambda h, kc=kc, tt=tt, pbank=pbank: h.activation(
                        c_tok[:, tt, kc * 128:(kc + 1) * 128], pbank[:, 0:128], AF.Copy),
                        reads=[pb], writes=[B("ctok")])
            x1_v = dr[f"x1_sp{j}"].rearrange("(tt p) d -> p tt d", p=128)
            ln2_v = dr[f"ln2_sp{j}"].rearrange("(tt p) d -> p tt d", p=128)
            xqT_v = dr[f"xqT_sp{j}"].rearrange("(kc p) t -> p kc t", p=128)
            for tt in range(4):
                ld("sp", "cx", c_x, x_own_v[:, j, tt, :], B("cx"))
                ln_stats(c_x, B("cx"), C_EPS5)
                s.op("act", lambda h: h.activation(
                    c_xn2, c_x, AF.Identity,
                    scale=c_stat[:, 5:6], bias=c_stat[:, 8:9]),
                    reads=[B("cx"), B("cst")], writes=[B("cxn2")])
                s.op("dve", lambda h, tt=tt: h.tensor_tensor(
                    c_tok[:, tt, :], c_tok[:, tt, :], c_xn2, ALU.add),
                    reads=[B("ctok"), B("cxn2")], writes=[B("ctok")])
                ln_stats(c_tok[:, tt, :], B("ctok"), C_EPS5)
                s.op("act", lambda h, tt=tt: h.activation(
                    c_xn2, c_tok[:, tt, :], AF.Identity,
                    scale=c_stat[:, 5:6], bias=c_stat[:, 8:9]),
                    reads=[B("ctok"), B("cst")], writes=[B("cxn2")])
                s.op("dve", lambda h: h.tensor_tensor(c_x, c_x, c_xn2, ALU.add),
                     reads=[B("cx"), B("cxn2")], writes=[B("cx")])
                st("act", "cx1", x1_v[:, tt, :], c_x, B("cx"), wbufs=[B(f"x1s{j}")])
                ln_stats(c_x, B("cx"), C_EPS5)
                s.op("act", lambda h: h.activation(
                    c_ln2, c_x, AF.Identity,
                    scale=c_stat[:, 5:6], bias=c_stat[:, 8:9]),
                    reads=[B("cx"), B("cst")], writes=[B("cln2")])
                st("act", "cln2", ln2_v[:, tt, :], c_ln2, B("cln2"), wbufs=[B(f"ln2s{j}")])
                s.op("act", lambda h: h.activation(c_stat[:, 6:7], c_stat[:, 5:6],
                                                   AF.Square),
                     reads=[B("cst")], writes=[B("cst")])
                s.op("dve", lambda h: h.tensor_tensor(c_stat[:, 6:7], c_stat[:, 6:7],
                                                      c_stat[:, 3:4], ALU.mult),
                     reads=[B("cst")], writes=[B("cst")])
                s.op("act", lambda h: h.activation(
                    c_stat[:, 6:7], c_stat[:, 6:7], AF.Ln,
                    bias=sb_consts[:, C_EPS6:C_EPS6 + 1]),
                    reads=[B("cst"), B("consts")], writes=[B("cst")])
                s.op("act", lambda h: h.activation(c_stat[:, 6:7], c_stat[:, 6:7], AF.Exp,
                                                   scale=-0.5),
                     reads=[B("cst")], writes=[B("cst")])
                s.op("act", lambda h: h.activation(
                    c_xn2, c_ln2, AF.Identity, scale=c_stat[:, 6:7]),
                    reads=[B("cln2"), B("cst")], writes=[B("cxn2")])
                s.op("dve", lambda h: h.tensor_tensor(c_xn2, c_xn2, sb_fnw, ALU.mult),
                     reads=[B("cxn2"), B("fnw")], writes=[B("cxn2")])
                s.op("dve", lambda h: h.tensor_reduce(
                    c_stat[:, 9:10], c_xn2, mybir.AxisListType.X, ALU.max,
                    apply_absolute_value=True),
                    reads=[B("cxn2")], writes=[B("cst")])
                s.op("dve", lambda h: h.tensor_scalar(c_stat[:, 9:10], c_stat[:, 9:10],
                                                      1e-8, None, ALU.max),
                     reads=[B("cst")], writes=[B("cst")])
                s.op("dve", lambda h: h.reciprocal(c_stat[:, 10:11], c_stat[:, 9:10]),
                     reads=[B("cst")], writes=[B("cst")])
                s.op("dve", lambda h: h.tensor_scalar(c_stat[:, 11:12], c_stat[:, 9:10],
                                                      1.0 / 127.0, None, ALU.mult),
                     reads=[B("cst")], writes=[B("cst")])
                s.op("dve", lambda h: h.tensor_scalar(c_stat[:, 10:11], c_stat[:, 10:11],
                                                      127.0, None, ALU.mult),
                     reads=[B("cst")], writes=[B("cst")])
                s.op("dve", lambda h: h.tensor_scalar(c_xn2, c_xn2,
                                                      c_stat[:, 10:11], None, ALU.mult),
                     reads=[B("cxn2"), B("cst")], writes=[B("cxn2")])
                s.op("dve", lambda h: h.tensor_scalar(c_xn2, c_xn2, MAGIC, MAGIC,
                                                      ALU.add, ALU.subtract),
                     reads=[B("cxn2")], writes=[B("cxn2")])
                s.op("dve", lambda h: h.tensor_scalar(c_xq, c_xn2, 127.0, -127.0,
                                                      ALU.min, ALU.max),
                     reads=[B("cxn2")], writes=[B("cxq")])
                for kc in range(16):
                    pg = (psG, psH)[kc % 2]
                    pgb = B(f"psT{kc % 2}")
                    s.op("pe", lambda h, kc=kc, pg=pg: h.transpose(
                        pg[:, 0:128], c_xq[:, kc * 128:(kc + 1) * 128], sb_ident_h),
                        reads=[B("cxq"), B("identh")], writes=[pgb])
                    s.op("act", lambda h, kc=kc, pg=pg: h.activation(
                        c_xqT[:, kc, :], pg[:, 0:128], AF.Copy),
                        reads=[pgb], writes=[B("cxqT")])
                st("act", "cxq", xqT_v[:, :, tt * 128:(tt + 1) * 128], c_xqT,
                   B("cxqT"), wbufs=[B(f"xqs{j}")])
                s.op("pe", lambda h: h.transpose(
                    psE[0:1, 0:128], c_stat[:, 11:12], sb_ident_f),
                    reads=[B("cst"), B("identf")], writes=[B("psE")])
                s.op("act", lambda h, tt=tt: h.activation(
                    c_sxr[0:1, tt * 128:(tt + 1) * 128], psE[0:1, 0:128], AF.Copy),
                    reads=[B("psE")], writes=[B("csxr")])
            st("act", "csx", dr[f"sx_sp{j}"], c_sxr, B("csxr"), wbufs=[B(f"sxs{j}")])
        if DEV_DBG:
            s.dma("sp", "dbg", lambda h: h.dma_start(
                out=dbg["dbg_x1"], in_=dr["x1_sp0"]),
                reads=[B("x1s0")], writes=[B("dbgx1")])
            s.dma("sp", "dbg", lambda h: h.dma_start(
                out=dbg["dbg_rs"], in_=dr["rs_dst0"]),
                reads=[B("rsd0")], writes=[B("dbgrs")])
            s.dma("sp", "dbg", lambda h: h.dma_start(
                out=dbg["dbg_xq"].rearrange("(kc p) t -> p kc t", p=128),
                in_=dr["xqT_sp0"].rearrange("(kc p) t -> p kc t", p=128)),
                reads=[B("xqs0")], writes=[B("dbgxq")])

        s.barrier()
        es_c.close()
        es_f = ExitStack()
        def sbf(name, shape, dt=F32):
            return es_f.enter_context(nc.sbuf_tensor(name, shape, dt)).ap()
        f_h = sbf("f_h", [128, 64, BLK], BF16)
        f_xq = sbf("f_xq", [128, 16, BLK], BF16)
        f_w = sbf("f_w", [128, 2, 16, 128], BF16)
        f_wd = sbf("f_wd", [128, 2, 64, 128], BF16)
        f_sxs = sbf("f_sxs", [128, BLK])
        f_scr = sbf("f_scr", [128, BLK])
        f_dn = sbf("f_dn", [128, 2, BLK])
        f_tok = sbf("f_tok", [128, 4, D])
        f_ln = sbf("f_ln", [128, D])
        f_res = sbf("f_res", [128, D])
        f_sscr = sbf("f_sscr", [128, D], BF16)
        f_stat = sbf("f_stat", [128, 12])
        f_srow = sbf("f_srow", [1, BLK])
        sb_sdg = sbf("sb_sdg", [128, 16])
        s.op("dve", lambda h: h.tensor_scalar(sb_sdg, sb_fg, sb_rcol[:, 1:2],
                                              None, ALU.mult),
             reads=[B("fg"), B("rcol")], writes=[B("sdg")])

        def f_ln_stats(tile_ap, tilebuf, eps_col):
            s.op("act", lambda h: h.activation(f_sscr, tile_ap, AF.Copy,
                                               accum_out=f_stat[:, 0:1]),
                 reads=[tilebuf], writes=[B("fscr2"), B("fst")])
            s.op("act", lambda h: h.activation(f_sscr, tile_ap, AF.Square,
                                               accum_out=f_stat[:, 1:2]),
                 reads=[tilebuf], writes=[B("fscr2"), B("fst")])
            s.op("dve", lambda h: h.tensor_scalar(f_stat[:, 2:3], f_stat[:, 0:1],
                                                  1.0 / D, None, ALU.mult),
                 reads=[B("fst")], writes=[B("fst")])
            s.op("dve", lambda h: h.tensor_scalar(f_stat[:, 3:4], f_stat[:, 1:2],
                                                  1.0 / D, None, ALU.mult),
                 reads=[B("fst")], writes=[B("fst")])
            s.op("dve", lambda h: h.tensor_scalar(f_stat[:, 4:5], f_stat[:, 2:3],
                                                  f_stat[:, 2:3], None, ALU.mult),
                 reads=[B("fst")], writes=[B("fst")])
            s.op("dve", lambda h: h.tensor_tensor(f_stat[:, 3:4], f_stat[:, 3:4],
                                                  f_stat[:, 4:5], ALU.subtract),
                 reads=[B("fst")], writes=[B("fst")])
            s.op("act", lambda h, eps_col=eps_col: h.activation(
                f_stat[:, 5:6], f_stat[:, 3:4], AF.Ln,
                bias=sb_consts[:, eps_col:eps_col + 1]),
                 reads=[B("fst"), B("consts")], writes=[B("fst")])
            s.op("act", lambda h: h.activation(f_stat[:, 5:6], f_stat[:, 5:6], AF.Exp,
                                               scale=-0.5),
                 reads=[B("fst")], writes=[B("fst")])
            s.op("dve", lambda h: h.tensor_scalar(f_stat[:, 8:9], f_stat[:, 2:3],
                                                  f_stat[:, 5:6], -1.0, ALU.mult, ALU.mult),
                 reads=[B("fst")], writes=[B("fst")])

        out_v = out_own.rearrange("(j tt p) d -> p j tt d", p=128, j=NR)
        for j in range(NR):
            xqT_v = dr[f"xqT_sp{j}"].rearrange("(kc p) t -> p kc t", p=128)
            ld("sp", "fxq", f_xq, xqT_v, B("fxq"), rbufs=[B(f"xqs{j}")])
            ld("sp", "fsx", f_srow, dr[f"sx_sp{j}"], B("fsrow"), rbufs=[B(f"sxs{j}")])
            s.op("dve", lambda h: h.tensor_scalar(f_srow, f_srow,
                                                  sb_scales[0:1, 0:1], None, ALU.mult),
                 reads=[B("fsrow"), B("scales")], writes=[B("fsrow")])
            s.op("pe", lambda h: h.matmul(psE, sb_ones, f_srow, start=True, stop=True),
                 reads=[B("fsrow"), B("ones")], writes=[B("psE")])
            s.op("dve", lambda h: h.tensor_copy(f_sxs, psE),
                 reads=[B("psE")], writes=[B("fsxs")])
            for mh in range(64):
                par = mh % 2
                r0, col0 = mh // 8, (mh % 8) * 128
                srcw = dr["wqu_dst"][r0 * D:(r0 + 1) * D, col0:col0 + 128]
                fwb = B(f"fw{par}")
                s.dma("sp", f"fw{par}", lambda h, par=par, srcw=srcw: h.dma_start(
                    out=f_w[:, par, :, :],
                    in_=srcw.rearrange("(kc p) m -> p kc m", p=128)),
                    reads=[B("wqu")], writes=[fwb])
                pbank, pb = ((psC, B("psC")), (psD, B("psD")))[par]
                pairs = [(f_w[:, par, kc, :], f_xq[:, kc, :]) for kc in range(16)]
                mmgroup(pbank, pairs, pb, [fwb, B("fxq")])
                s.op("dve", lambda h, par=par, pbank=pbank: h.tensor_tensor(
                    f_scr, pbank, f_sxs, ALU.mult),
                    reads=[pb, B("fsxs")], writes=[B("fscr")])
                s.op("act", lambda h, mh=mh: h.activation(
                    f_h[:, mh, :], f_scr, AF.Silu),
                    reads=[B("fscr")], writes=[B("fh")])
            for dt16 in range(16):
                par = dt16 % 2
                r0 = dt16 // 2
                col0 = (dt16 % 2) * 128
                srcd = dr["wqd_dst"][r0 * 4 * D:(r0 + 1) * 4 * D, col0:col0 + 128]
                fdb = B(f"fwd{par}")
                s.dma("sp", f"fwd{par}", lambda h, par=par, srcd=srcd: h.dma_start(
                    out=f_wd[:, par, :, :],
                    in_=srcd.rearrange("(kh p) m -> p kh m", p=128)),
                    reads=[B("wqd")], writes=[fdb])
                pbank, pb = ((psA, B("psA")), (psB, B("psB")))[par]
                pairs = [(f_wd[:, par, kh, :], f_h[:, kh, :]) for kh in range(64)]
                mmgroup(pbank, pairs, pb, [fdb, B("fh")])
                s.op("dve", lambda h, par=par, dt16=dt16, pbank=pbank: h.tensor_scalar(
                    f_dn[:, par, :], pbank, sb_sdg[:, dt16:dt16 + 1], None, ALU.mult),
                    reads=[pb, B("sdg")], writes=[B(f"fdn{par}")])
                for tt in range(4):
                    tpar = tt % 2
                    pe_b, peb = ((psE, B("psE")), (psF, B("psF")))[tpar]
                    s.op("pe", lambda h, par=par, tt=tt, pe_b=pe_b: h.transpose(
                        pe_b[:, 0:128], f_dn[:, par, tt * 128:(tt + 1) * 128], sb_ident_f),
                        reads=[B(f"fdn{par}"), B("identf")], writes=[peb])
                    s.op("act", lambda h, tt=tt, dt16=dt16, pe_b=pe_b: h.activation(
                        f_tok[:, tt, dt16 * 128:(dt16 + 1) * 128], pe_b[:, 0:128],
                        AF.Copy), reads=[peb], writes=[B("ftok")])
            ln2_v = dr[f"ln2_sp{j}"].rearrange("(tt p) d -> p tt d", p=128)
            x1_v = dr[f"x1_sp{j}"].rearrange("(tt p) d -> p tt d", p=128)
            for tt in range(4):
                ld("sp", "fres", f_res, ln2_v[:, tt, :], B("fres"), rbufs=[B(f"ln2s{j}")])
                s.op("dve", lambda h, tt=tt: h.tensor_tensor(
                    f_tok[:, tt, :], f_tok[:, tt, :], f_res, ALU.add),
                    reads=[B("ftok"), B("fres")], writes=[B("ftok")])
                f_ln_stats(f_tok[:, tt, :], B("ftok"), C_EPS5)
                s.op("act", lambda h, tt=tt: h.activation(
                    f_ln, f_tok[:, tt, :], AF.Identity,
                    scale=f_stat[:, 5:6], bias=f_stat[:, 8:9]),
                    reads=[B("ftok"), B("fst")], writes=[B("fln")])
                ld("sp", "fres", f_res, x1_v[:, tt, :], B("fres"), rbufs=[B(f"x1s{j}")])
                s.op("dve", lambda h: h.tensor_tensor(f_res, f_res, f_ln, ALU.add),
                     reads=[B("fres"), B("fln")], writes=[B("fres")])
                st("act", "fout", out_v[:, j, tt, :], f_res, B("fres"),
                   wbufs=[B("outw")])

        es_f.close()
        s.wait_all_done("act")
        s.wait_all_done("sp")
        s.build(block)
    return nc, list(I.keys()), dbg


def _host_inputs(inputs):
    x = np.asarray(inputs["L0_attn_x"])
    W_in = np.asarray(inputs["L0_attn_in_proj_w"])
    conv_w = np.asarray(inputs["L0_attn_conv_w"])
    conv_b = np.asarray(inputs["L0_attn_conv_b"])
    xproj = np.asarray(inputs["L0_attn_x_proj_w"])
    dt_w = np.asarray(inputs["L0_attn_dt_w"])
    dt_b = np.asarray(inputs["L0_attn_dt_b"])
    A_log = np.asarray(inputs["L0_attn_A_log"])
    D_param = np.asarray(inputs["L0_attn_D_param"])
    W_out = np.asarray(inputs["L0_attn_out_proj_w"])
    W_up = np.asarray(inputs["L0_ffn_weight_up"])
    W_dn = np.asarray(inputs["L0_ffn_weight_down"])
    anorm = np.asarray(inputs["L0_attn_norm_w"])
    fnorm = np.asarray(inputs["L0_ffn_norm_weight"])
    fgamma = np.asarray(inputs["L0_ffn_gamma"])

    import ml_dtypes
    ident_h = np.eye(128, dtype=ml_dtypes.bfloat16)
    ident_f = np.eye(128, dtype=np.float32)
    ones_row = np.ones((1, 128), np.float32)
    consts = np.zeros((128, C_NCONST), np.float32)
    consts[:, C_ONE] = 1.0
    consts[:, C_EPS5] = 1e-5
    consts[:, C_EPS6] = 1e-6
    consts[:, C_EPS8] = 1e-8

    sel_bc_h = np.zeros((32, 32, 128), ml_dtypes.bfloat16)
    for si in range(32):
        sel_bc_h[si, si, :] = 1.0
    sel_dt_h = np.zeros((33, 128), np.float32)
    sel_dt_h[32, :] = 1.0

    woutT = np.ascontiguousarray(W_out.T)
    wupT = np.ascontiguousarray(W_up.T)
    wdnT = np.ascontiguousarray(W_dn.T)

    maps = []
    for r in range(8):
        b, g = divmod(r, 4)
        xb = x[b]
        x_own_a = np.ascontiguousarray(np.concatenate(
            [xb[j * 2048 + g * 512: j * 2048 + (g + 1) * 512] for j in range(NR)], axis=0))
        winT_a = np.ascontiguousarray(np.concatenate(
            [W_in[g * 1024:(g + 1) * 1024], W_in[4096 + g * 1024:4096 + (g + 1) * 1024]],
            axis=0).T)
        maps.append({
            "x_own": x_own_a,
            "winT": winT_a,
            "woutT": np.ascontiguousarray(woutT[g * 1024:(g + 1) * 1024, :]),
            "xprojT": np.ascontiguousarray(xproj[:, g * 1024:(g + 1) * 1024].T),
            "conv_w": np.ascontiguousarray(
                conv_w[g * 1024:(g + 1) * 1024, 0, :].reshape(8, 128, 4).transpose(1, 0, 2)),
            "conv_b": np.ascontiguousarray(
                conv_b[g * 1024:(g + 1) * 1024].reshape(8, 128).T),
            "dt_w": np.ascontiguousarray(
                dt_w[g * 1024:(g + 1) * 1024, 0].reshape(8, 128).T),
            "dt_b": np.ascontiguousarray(
                dt_b[g * 1024:(g + 1) * 1024].reshape(8, 128).T),
            "a_log": np.ascontiguousarray(
                A_log[g * 1024:(g + 1) * 1024].reshape(8, 128, S).transpose(1, 0, 2)),
            "d_prm": np.ascontiguousarray(
                D_param[g * 1024:(g + 1) * 1024].reshape(8, 128).T),
            "anorm_w": np.ascontiguousarray(anorm.reshape(16, 128).T),
            "fnorm_bc": np.ascontiguousarray(np.tile(fnorm[None, :], (128, 1))),
            "fgam": np.ascontiguousarray(fgamma.reshape(16, 128).T),
            "wupT_sl": np.ascontiguousarray(wupT[:, r * 1024:(r + 1) * 1024]),
            "wdnT_sl": np.ascontiguousarray(wdnT[:, r * 256:(r + 1) * 256]),
            "ident_h": ident_h,
            "ident_f": ident_f,
            "ones_row": ones_row,
            "ones_col": np.ones((128, 1), np.float32),
            "sel_bc": sel_bc_h, "sel_dt": sel_dt_h,
            "consts": consts,
        })
    return maps


def kernel(**inputs):
    nc, in_names, dbg = build_program()
    maps = _host_inputs(inputs)
    res = run_bass_kernel_spmd(nc, maps, core_ids=list(range(8)))
    out = np.zeros((2, T, D), np.float32)
    for r in range(8):
        b, g = divmod(r, 4)
        if "out_own" in res.results[r]:
            oo = res.results[r]["out_own"]
            for j in range(NR):
                out[b, j * 2048 + g * 512: j * 2048 + (g + 1) * 512] = \
                    oo[j * BLK:(j + 1) * BLK]
    return out


# version 11
# speedup vs baseline: 1.1365x; 1.1365x over previous
from contextlib import ExitStack

import numpy as np
import concourse.bass as bass
import concourse.mybir as mybir
from concourse.bass_utils import run_bass_kernel_spmd


ENGINES = ("pe", "act", "dve", "pool", "sp")


class Buf:
    __slots__ = ("name", "last_write", "readers")

    def __init__(self, name):
        self.name = name
        self.last_write = None
        self.readers = []


class Sched:
    def __init__(self, nc: bass.Bass, es: ExitStack):
        self.nc = nc
        self.es = es
        self.sem = {e: es.enter_context(nc.semaphore(f"sem_{e}")) for e in ENGINES}
        self.cnt = {e: 0 for e in ENGINES}
        self.dsem = {}
        self.dcnt = {}
        self.watermark = {e: {} for e in ENGINES}
        self.items = []
        self._bufs = {}

    def buf(self, name):
        b = self._bufs.get(name)
        if b is None:
            b = self._bufs[name] = Buf(name)
        return b

    def _stream_sem(self, stream):
        s = self.dsem.get(stream)
        if s is None:
            s = self.dsem[stream] = self.es.enter_context(
                self.nc.semaphore(f"dsem_{stream}"))
            self.dcnt[stream] = 0
        return s

    def _collect_waits(self, queue, reads, writes):
        need = {}

        def add(tok):
            if tok is None:
                return
            kind, src, v = tok
            if kind == "d":
                v = self.dcnt[src]
            k = (kind, src)
            if need.get(k, 0) < v:
                need[k] = v

        for b in reads:
            add(b.last_write)
        for b in writes:
            add(b.last_write)
            for t in b.readers:
                add(t)
        waits = []
        wm = self.watermark[queue]
        for k in sorted(need, key=str):
            v = need[k]
            if wm.get(k, 0) >= v:
                continue
            wm[k] = v
            kind, src = k
            sem = self.sem[src] if kind == "c" else self.dsem[src]
            waits.append((sem, v))
        return waits

    def _record(self, tok, reads, writes):
        for b in reads:
            b.readers.append(tok)
        for b in writes:
            b.last_write = tok
            b.readers = []

    def op(self, queue, emit_fn, reads=(), writes=()):
        waits = self._collect_waits(queue, reads, writes)
        self.cnt[queue] += 1
        tok = ("c", queue, self.cnt[queue])
        self.items.append((queue, emit_fn, waits, (self.sem[queue], 1)))
        self._record(tok, reads, writes)
        return tok

    def dma(self, queue, stream, emit_fn, reads=(), writes=()):
        sem = self._stream_sem(stream)
        waits = self._collect_waits(queue, reads, writes)
        self.dcnt[stream] += 16
        tok = ("d", stream, self.dcnt[stream])
        self.items.append((queue, emit_fn, waits, (sem, 16)))
        self._record(tok, reads, writes)
        return tok

    def cc(self, stream, emit_fn, reads=(), writes=(), inc=1):
        sem = self._stream_sem(stream)
        waits = self._collect_waits("pool", reads, writes)
        self.dcnt[stream] += inc
        tok = ("d", stream, self.dcnt[stream])
        self.items.append(("pool", emit_fn, waits, (sem, inc)))
        self._record(tok, reads, writes)
        return tok

    def raw(self, queue, emit_fn):
        self.items.append((queue, emit_fn, [], None))

    def barrier(self):
        targets = []
        for e in ENGINES:
            if self.cnt[e]:
                targets.append(("c", e, self.cnt[e]))
        for stream, c in self.dcnt.items():
            if c:
                targets.append(("d", stream, c))
        for e in ENGINES:
            waits = []
            wm = self.watermark[e]
            for kind, src, v in targets:
                k = (kind, src)
                if kind == "c" and src == e:
                    continue
                if wm.get(k, 0) >= v:
                    continue
                wm[k] = v
                sem = self.sem[src] if kind == "c" else self.dsem[src]
                waits.append((sem, v))

            def emit(h, waits=waits):
                for sem, v in waits:
                    h.wait_ge(sem, v)
                return None

            if waits:
                self.items.append((e, emit, [], None))

    def wait_all_done(self, queue):
        targets = []
        for e in ENGINES:
            if self.cnt[e]:
                targets.append((self.sem[e], self.cnt[e]))
        for stream, c in self.dcnt.items():
            if c:
                targets.append((self.dsem[stream], c))

        def emit(h, targets=targets):
            for sem, v in targets:
                h.wait_ge(sem, v)
            return None

        self.items.append((queue, emit, [], None))

    def build(self, block):
        per_engine = {e: [] for e in ENGINES}
        for it in self.items:
            per_engine[it[0]].append(it)

        def mk_section(items):
            def section(h):
                for _, emit_fn, waits, inc in items:
                    for sem, v in waits:
                        h.wait_ge(sem, v)
                    inst = emit_fn(h)
                    if inc is not None:
                        sem, amt = inc
                        assert inst is not None, "emit_fn must return its instruction"
                        inst.then_inc(sem, amt)
            return section

        if per_engine["pe"]:
            block.tensor(mk_section(per_engine["pe"]))
        if per_engine["act"]:
            block.scalar(mk_section(per_engine["act"]))
        if per_engine["dve"]:
            block.vector(mk_section(per_engine["dve"]))
        if per_engine["pool"]:
            block.gpsimd(mk_section(per_engine["pool"]))
        if per_engine["sp"]:
            block.sync(mk_section(per_engine["sp"]))


F32 = mybir.dt.float32
BF16 = mybir.dt.bfloat16
AF = mybir.ActivationFunctionType
ALU = mybir.AluOpType

D = 2048
DI = 4096
DIS = 1024
S = 16
T = 8192
NR = 4
TS = 2048
BLK = 512
NT = NR * BLK
TSC = 1024
HT = 512
MAGIC = 1.5 * 2 ** 23
G4 = [[0, 1, 2, 3], [4, 5, 6, 7]]
G8 = [[0, 1, 2, 3, 4, 5, 6, 7]]

C_ONE = 0
C_EPS5 = 1
C_EPS6 = 2
C_EPS8 = 3
C_NCONST = 4

PHASES = "WAB"
DEV_DBG = False


def build_program():
    nc = bass.Bass(target_bir_lowering=False, debug=False)

    I = {}

    def inp(name, shape, dt=F32):
        I[name] = nc.dram_tensor(name, shape, dt, kind="ExternalInput").ap()
        return I[name]

    x_own = inp("x_own", [NT, D])
    winT = inp("winT", [D, 2 * DIS])
    woutT = inp("woutT", [DIS, D])
    xprojT = inp("xprojT", [DIS, 33])
    conv_w = inp("conv_w", [128, 8, 4])
    conv_b = inp("conv_b", [128, 8])
    dt_w = inp("dt_w", [128, 8])
    dt_b = inp("dt_b", [128, 8])
    a_log = inp("a_log", [128, 8, S])
    d_prm = inp("d_prm", [128, 8])
    anorm_w = inp("anorm_w", [128, 16])
    fnorm_bc = inp("fnorm_bc", [128, D])
    fgam = inp("fgam", [128, 16])
    wupT_sl = inp("wupT_sl", [D, 1024])
    wdnT_sl = inp("wdnT_sl", [4 * D, 256])
    ident_h = inp("ident_h", [128, 128], BF16)
    ident_f = inp("ident_f", [128, 128])
    ones_row = inp("ones_row", [1, 128])
    ones_col = inp("ones_col", [128, 1])
    sel_bc = inp("sel_bc", [32, 32, 128], BF16)
    sel_dt = inp("sel_dt", [33, 128])
    consts = inp("consts", [128, C_NCONST])

    out_own = nc.dram_tensor("out_own", [NT, D], F32, kind="ExternalOutput").ap()
    dbg = {}
    if DEV_DBG:
        def dbgout(name, shape, dt=F32):
            dbg[name] = nc.dram_tensor(name, shape, dt, kind="ExternalOutput").ap()
        dbgout("dbg_xn", [D, BLK], BF16)
        dbgout("dbg_ssm", [33, TS])
        dbgout("dbg_y", [DIS, TS], BF16)
        dbgout("dbg_xc", [8, 128, TS], BF16)
        dbgout("dbg_sz", [8, 128, TS], BF16)
        dbgout("dbg_ssp", [33, TS])
        dbgout("dbg_x1", [BLK, D])
        dbgout("dbg_rs", [D, BLK])
        dbgout("dbg_xq", [D, BLK], BF16)

    dr = {}

    def scratch(name, shape, dt):
        dr[name] = nc.dram_tensor(name, shape, dt).ap()
        return dr[name]

    for j in range(NR):
        scratch(f"agx_src{j}", [D, BLK], BF16)
        scratch(f"agx_dst{j}", [4 * D, BLK], BF16)
        scratch(f"ars_src{j}", [33, TS], F32)
        scratch(f"ars_dst{j}", [33, TS], F32)
        scratch(f"rs_src{j}", [4 * D, BLK], F32)
        scratch(f"rs_dst{j}", [D, BLK], F32)
        scratch(f"x1_sp{j}", [BLK, D], F32)
        scratch(f"ln2_sp{j}", [BLK, D], F32)
        scratch(f"xqT_sp{j}", [D, BLK], BF16)
        scratch(f"sx_sp{j}", [1, BLK], F32)
    scratch("sp_dt", [8, 128, T], BF16)
    scratch("sp_xc", [8, 128, T], BF16)
    scratch("sp_sz", [8, 128, T], BF16)
    scratch("win_t", [16, 128, 16, 128], BF16)
    scratch("wout_t", [16, 128, 8, 128], BF16)
    scratch("wqu_src", [8, 128, 16, 128], BF16)
    scratch("wqu_dst", [64, 128, 16, 128], BF16)
    scratch("wqd_src", [2, 128, 64, 128], BF16)
    scratch("wqd_dst", [16, 128, 64, 128], BF16)
    scratch("sc_src", [1, 2], F32)
    scratch("sc_dst", [1, 2], F32)

    with ExitStack() as es:
        def sb(name, shape, dt=F32):
            return es.enter_context(nc.sbuf_tensor(name, shape, dt)).ap()

        def psum(name, shape, dt=F32):
            return es.enter_context(nc.psum_tensor(name, shape, dt)).ap()

        sb_consts = sb("sb_consts", [128, C_NCONST])
        sb_ident_h = sb("sb_ident_h", [128, 128], BF16)
        sb_ident_f = sb("sb_ident_f", [128, 128])
        sb_ones = sb("sb_ones", [1, 128])
        sb_onesc = sb("sb_onesc", [128, 1])
        sb_sel = sb("sb_sel", [32, 32, 128], BF16)
        sb_seldt = sb("sb_seldt", [33, 128])
        sb_convw = sb("sb_convw", [128, 8, 4])
        sb_convb = sb("sb_convb", [128, 8])
        sb_dtw = sb("sb_dtw", [128, 8])
        sb_dtb = sb("sb_dtb", [128, 8])
        sb_A = sb("sb_A", [128, 8, S])
        sb_D = sb("sb_D", [128, 8])
        sb_anw = sb("sb_anw", [128, 16])
        sb_fg = sb("sb_fg", [128, 16])
        sb_carry = sb("sb_carry", [128, 8, S])
        sb_scales = sb("sb_scales", [1, 4])
        sb_rcol = sb("sb_rcol", [128, 4])
        sb_xpj = sb("sb_xpj", [128, 8 * 33], BF16)
        psA = psum("psA", [128, 512])
        psB = psum("psB", [128, 512])
        psC = psum("psC", [128, 512])
        psD = psum("psD", [128, 512])
        psE = psum("psE", [128, 512])
        psF = psum("psF", [128, 512])
        psG = psum("psG", [128, 1024], BF16)
        psH = psum("psH", [128, 1024], BF16)

        block = es.enter_context(nc.Block())
        s = Sched(nc, es)
        B = s.buf

        es_wa = ExitStack()
        def sbwa(name, shape, dt=F32):
            return es_wa.enter_context(nc.sbuf_tensor(name, shape, dt)).ap()
        w_ld = sbwa("w_ld", [128, 2, 2048])
        w_cast = sbwa("w_cast", [128, 2, 2048], BF16)
        a_x = sbwa("a_x", [128, D])
        a_scr = sbwa("a_scr", [128, D], BF16)
        a_xn = sbwa("a_xn", [128, D], BF16)
        a_stat = sbwa("a_stat", [128, 12])
        a_feat = sbwa("a_feat", [128, 16, 128], BF16)

        def ld(queue, stream, dst_ap, src_ap, wbuf, rbufs=()):
            return s.dma(queue, stream, lambda h: h.dma_start(out=dst_ap, in_=src_ap),
                         reads=list(rbufs), writes=[wbuf])

        def st(queue, stream, dst_ap, src_ap, rbuf, wbufs=()):
            return s.dma(queue, stream, lambda h: h.dma_start(out=dst_ap, in_=src_ap),
                         reads=[rbuf], writes=list(wbufs))

        def mmgroup(psum_ap, pairs, pbuf, rbufs):
            def emit(h):
                inst = None
                n = len(pairs)
                for i, (lt, rh) in enumerate(pairs):
                    inst = h.matmul(psum_ap, lt, rh, start=(i == 0), stop=(i == n - 1))
                return inst
            return s.op("pe", emit, reads=list(rbufs), writes=[pbuf])

        ld("sp", "ldc", sb_consts, consts, B("consts"))
        ld("sp", "ldc", sb_ident_h, ident_h, B("identh"))
        ld("sp", "ldc", sb_ident_f, ident_f, B("identf"))
        ld("sp", "ldc", sb_ones, ones_row, B("ones"))
        ld("sp", "ldc", sb_onesc, ones_col, B("onesc"))
        ld("sp", "ldc", sb_sel, sel_bc, B("sel"))
        ld("sp", "ldc", sb_seldt, sel_dt, B("seldt"))
        ld("sp", "ldc", sb_convw, conv_w, B("convw"))
        ld("sp", "ldc", sb_convb, conv_b, B("convb"))
        ld("sp", "ldc", sb_dtw, dt_w, B("dtw"))
        ld("sp", "ldc", sb_dtb, dt_b, B("dtb"))
        ld("sp", "ldc", sb_A, a_log, B("A"))
        ld("sp", "ldc", sb_D, d_prm, B("D"))
        ld("sp", "ldc", sb_anw, anorm_w, B("anw"))
        ld("sp", "ldc", sb_fg, fgam, B("fg"))
        ld("sp", "ldc", w_ld[:, 0, 0:8 * 33].rearrange("p (k m) -> p k m", k=8),
           xprojT.rearrange("(k p) m -> p k m", p=128), B("wld0"))
        s.op("act", lambda h: h.activation(sb_A, sb_A, AF.Exp), reads=[B("A")], writes=[B("A")])
        s.op("dve", lambda h: h.tensor_scalar(sb_A, sb_A, -1.0, None, ALU.mult),
             reads=[B("A")], writes=[B("A")])
        s.op("dve", lambda h: h.tensor_copy(sb_xpj, w_ld[:, 0, 0:8 * 33]),
             reads=[B("wld0")], writes=[B("xpj")])

        x_own_v = x_own.rearrange("(j tt p) d -> p j tt d", p=128, j=NR)
        for j in range(NR):
            agx_src_v = dr[f"agx_src{j}"].rearrange("(kc p) t -> p kc t", p=128)
            for tt in range(4):
                xb = B("ax")
                ld("sp", "ax", a_x, x_own_v[:, j, tt, :], xb)
                s.op("act", lambda h: h.activation(
                    a_scr, a_x, AF.Copy, accum_out=a_stat[:, 0:1]),
                    reads=[xb], writes=[B("ascr"), B("ast")])
                s.op("act", lambda h: h.activation(
                    a_scr, a_x, AF.Square, accum_out=a_stat[:, 1:2]),
                    reads=[xb], writes=[B("ascr"), B("ast")])
                s.op("dve", lambda h: h.tensor_scalar(a_stat[:, 2:3], a_stat[:, 0:1],
                                                      1.0 / D, None, ALU.mult),
                     reads=[B("ast")], writes=[B("ast")])
                s.op("dve", lambda h: h.tensor_scalar(a_stat[:, 3:4], a_stat[:, 1:2],
                                                      1.0 / D, None, ALU.mult),
                     reads=[B("ast")], writes=[B("ast")])
                s.op("dve", lambda h: h.tensor_scalar(a_stat[:, 4:5], a_stat[:, 2:3],
                                                      a_stat[:, 2:3], None, ALU.mult),
                     reads=[B("ast")], writes=[B("ast")])
                s.op("dve", lambda h: h.tensor_tensor(a_stat[:, 3:4], a_stat[:, 3:4],
                                                      a_stat[:, 4:5], ALU.subtract),
                     reads=[B("ast")], writes=[B("ast")])
                s.op("act", lambda h: h.activation(a_stat[:, 5:6], a_stat[:, 3:4], AF.Ln,
                                                   bias=sb_consts[:, C_EPS5:C_EPS5 + 1]),
                     reads=[B("ast"), B("consts")], writes=[B("ast")])
                s.op("act", lambda h: h.activation(a_stat[:, 5:6], a_stat[:, 5:6], AF.Exp,
                                                   scale=-0.5),
                     reads=[B("ast")], writes=[B("ast")])
                s.op("act", lambda h: h.activation(a_stat[:, 6:7], a_stat[:, 5:6], AF.Square),
                     reads=[B("ast")], writes=[B("ast")])
                s.op("dve", lambda h: h.tensor_tensor(a_stat[:, 6:7], a_stat[:, 6:7],
                                                      a_stat[:, 3:4], ALU.mult),
                     reads=[B("ast")], writes=[B("ast")])
                s.op("act", lambda h: h.activation(a_stat[:, 6:7], a_stat[:, 6:7], AF.Ln,
                                                   bias=sb_consts[:, C_EPS5:C_EPS5 + 1]),
                     reads=[B("ast"), B("consts")], writes=[B("ast")])
                s.op("act", lambda h: h.activation(a_stat[:, 6:7], a_stat[:, 6:7], AF.Exp,
                                                   scale=-0.5),
                     reads=[B("ast")], writes=[B("ast")])
                s.op("dve", lambda h: h.tensor_tensor(a_stat[:, 7:8], a_stat[:, 5:6],
                                                      a_stat[:, 6:7], ALU.mult),
                     reads=[B("ast")], writes=[B("ast")])
                s.op("dve", lambda h: h.tensor_scalar(a_stat[:, 8:9], a_stat[:, 2:3],
                                                      a_stat[:, 7:8], -1.0, ALU.mult, ALU.mult),
                     reads=[B("ast")], writes=[B("ast")])
                s.op("act", lambda h: h.activation(
                    a_xn, a_x, AF.Identity,
                    scale=a_stat[:, 7:8], bias=a_stat[:, 8:9]),
                    reads=[xb, B("ast")], writes=[B("axn")])
                for kc in range(16):
                    pg = (psG, psH)[kc % 2]
                    pgb = B(f"psT{kc % 2}")
                    s.op("pe", lambda h, kc=kc, pg=pg: h.transpose(
                        pg[:, 0:128], a_xn[:, kc * 128:(kc + 1) * 128], sb_ident_h),
                        reads=[B("axn"), B("identh")], writes=[pgb])
                    s.op("dve", lambda h, kc=kc, pg=pg: h.tensor_scalar(
                        a_feat[:, kc, :], pg[:, 0:128], sb_anw[:, kc:kc + 1], None,
                        ALU.mult), reads=[pgb, B("anw")], writes=[B("afeat")])
                st("act", f"agx{j}", agx_src_v[:, :, tt * 128:(tt + 1) * 128],
                   a_feat, B("afeat"), wbufs=[B(f"agxs{j}")])
            s.cc(f"ccx{j}", lambda h, j=j: h.collective_compute(
                "AllGather", ALU.bypass, replica_groups=G4,
                ins=[dr[f"agx_src{j}"]], outs=[dr[f"agx_dst{j}"]]),
                reads=[B(f"agxs{j}")], writes=[B(f"agxd{j}")])
        if DEV_DBG:
            s.dma("sp", "dbg", lambda h: h.dma_start(
                out=dbg["dbg_xn"].rearrange("(kc p) t -> p kc t", p=128),
                in_=dr["agx_src0"].rearrange("(kc p) t -> p kc t", p=128)),
                reads=[B("agxs0")], writes=[B("dbgxn")])


        def cast_spill(src_ap, dst_ap, rows, cols, stream, dstbuf):
            nro = rows // 128
            nmt = cols // 128
            src_v = src_ap.rearrange("(r p) m -> p r m", p=128)
            dst_v = dst_ap.rearrange("mt p kc mm -> p mt kc mm")
            for r in range(nro):
                par = r % 2
                wb, cb = B(f"wld{par}"), B(f"wcast{par}")
                ld("sp", f"wld{par}", w_ld[:, par, 0:cols], src_v[:, r, :], wb)
                s.op("dve", lambda h, par=par, cols=cols: h.tensor_copy(
                    w_cast[:, par, 0:cols], w_ld[:, par, 0:cols]), reads=[wb], writes=[cb])
                st("act", stream, dst_v[:, :, r, :],
                   w_cast[:, par, 0:cols].rearrange("p (mt mm) -> p mt mm", mt=nmt),
                   cb, wbufs=[dstbuf])

        cast_spill(winT, dr["win_t"], D, 2 * DIS, "wspill", B("win_spill"))
        cast_spill(woutT, dr["wout_t"], DIS, D, "wspill", B("wout_spill"))

        def emit_quant():
            q_ld = es_b1.enter_context(nc.sbuf_tensor("q_ld", [128, 2, 512], F32)).ap()
            q_ct = es_b1.enter_context(nc.sbuf_tensor("q_ct", [128, 2, 512], BF16)).ap()
            q_acc = es_b1.enter_context(nc.sbuf_tensor("q_acc", [128, 1], F32)).ap()
            q_s = es_b1.enter_context(nc.sbuf_tensor("q_s", [128, 2], F32)).ap()
            s.op("dve", lambda h: h.memset(q_s, 0.0), writes=[B("qs")])

            def quant_abs(src_ap, rows, cols, col_idx):
                nro = rows // 128
                src_v = src_ap.rearrange("(r p) m -> p r m", p=128)
                i = 0
                for r in range(nro):
                    for c0 in range(0, cols, 512):
                        ce = min(c0 + 512, cols)
                        par = i % 2
                        i += 1
                        wb = B(f"qld{par}")
                        ld("sp", f"qld{par}", q_ld[:, par, 0:ce - c0], src_v[:, r, c0:ce], wb)
                        s.op("act", lambda h, par=par, n=ce - c0: h.activation(
                            q_ct[:, par, 0:n], q_ld[:, par, 0:n], AF.Abs,
                            accum_out=q_acc), reads=[wb], writes=[B(f"qct{par}"), B("qacc")])
                        s.op("dve", lambda h, col_idx=col_idx: h.tensor_tensor(
                            q_s[:, col_idx:col_idx + 1], q_s[:, col_idx:col_idx + 1],
                            q_acc, ALU.add), reads=[B("qacc"), B("qs")], writes=[B("qs")])

            quant_abs(wupT_sl, D, 1024, 0)
            quant_abs(wdnT_sl, 4 * D, 256, 1)
            s.op("pe", lambda h: h.matmul(psE[:1, 0:2], sb_onesc,
                                          q_s, start=True, stop=True),
                 reads=[B("qs"), B("onesc")], writes=[B("psE")])
            s.op("dve", lambda h: h.tensor_copy(sb_scales[:, 0:2], psE[:1, 0:2]),
                 reads=[B("psE")], writes=[B("scales")])
            st("act", "ccsc", dr["sc_src"], sb_scales[:, 0:2], B("scales"), wbufs=[B("scsrc")])
            s.cc("ccq", lambda h: h.collective_compute(
                "AllReduce", ALU.add, replica_groups=G8,
                ins=[dr["sc_src"]], outs=[dr["sc_dst"]]),
                reads=[B("scsrc")], writes=[B("scdst")])
            ld("sp", "ccsc", sb_scales[:, 0:2], dr["sc_dst"], B("scales"), rbufs=[B("scdst")])
            for ci in range(2):
                s.op("dve", lambda h, ci=ci: h.tensor_scalar(
                    sb_scales[:, ci:ci + 1], sb_scales[:, ci:ci + 1],
                    1.0 / (8192 * 2048), 1e-8, ALU.mult, ALU.max),
                    reads=[B("scales")], writes=[B("scales")])
            s.op("dve", lambda h: h.reciprocal(sb_scales[:, 2:4], sb_scales[:, 0:2]),
                 reads=[B("scales")], writes=[B("scales")])
            s.op("pe", lambda h: h.matmul(psE[:, 0:4], sb_ones, sb_scales[:1, 0:4],
                                          start=True, stop=True),
                 reads=[B("scales"), B("ones")], writes=[B("psE")])
            s.op("dve", lambda h: h.tensor_copy(sb_rcol, psE[:, 0:4]),
                 reads=[B("psE")], writes=[B("rcol")])

            def quant_pass(src_ap, dst_ap, rows, cols, col_idx, dstbuf):
                nro = rows // 128
                src_v = src_ap.rearrange("(r p) m -> p r m", p=128)
                dst_v = dst_ap.rearrange("mt p kc mm -> p mt kc mm")
                i = 0
                for r in range(nro):
                    for c0 in range(0, cols, 512):
                        ce = min(c0 + 512, cols)
                        par = i % 2
                        i += 1
                        wb, cb = B(f"qld{par}"), B(f"qct{par}")
                        ld("sp", f"qld{par}", q_ld[:, par, 0:ce - c0], src_v[:, r, c0:ce], wb)
                        s.op("dve", lambda h, par=par, n=ce - c0, col_idx=col_idx: h.tensor_scalar(
                            q_ld[:, par, 0:n], q_ld[:, par, 0:n],
                            sb_rcol[:, col_idx:col_idx + 1], None, ALU.mult),
                            reads=[wb, B("rcol")], writes=[wb])
                        s.op("dve", lambda h, par=par, n=ce - c0: h.tensor_scalar(
                            q_ld[:, par, 0:n], q_ld[:, par, 0:n], MAGIC, MAGIC,
                            ALU.add, ALU.subtract), reads=[wb], writes=[wb])
                        s.op("dve", lambda h, par=par, n=ce - c0: h.tensor_scalar(
                            q_ct[:, par, 0:n], q_ld[:, par, 0:n], 1.0, -1.0,
                            ALU.min, ALU.max), reads=[wb], writes=[cb])
                        nmt = (ce - c0) // 128
                        st("act", "qspill",
                           dst_v[:, c0 // 128:c0 // 128 + nmt, r, :],
                           q_ct[:, par, 0:ce - c0].rearrange("p (mt mm) -> p mt mm", mt=nmt),
                           cb, wbufs=[dstbuf])

            quant_pass(wupT_sl, dr["wqu_src"], D, 1024, 2, B("wqusrc"))
            quant_pass(wdnT_sl, dr["wqd_src"], 4 * D, 256, 3, B("wqdsrc"))
            s.cc("ccq", lambda h: h.collective_compute(
                "AllGather", ALU.bypass, replica_groups=G8,
                ins=[dr["wqu_src"]], outs=[dr["wqu_dst"]]),
                reads=[B("wqusrc")], writes=[B("wqu")])
            s.cc("ccq", lambda h: h.collective_compute(
                "AllGather", ALU.bypass, replica_groups=G8,
                ins=[dr["wqd_src"]], outs=[dr["wqd_dst"]]),
                reads=[B("wqdsrc")], writes=[B("wqd")])


        s.barrier()
        es_wa.close()
        es_b = ExitStack()
        es_b1 = ExitStack()
        def sbb(name, shape, dt=F32):
            return es_b.enter_context(nc.sbuf_tensor(name, shape, dt)).ap()
        def sbb1(name, shape, dt=F32):
            return es_b1.enter_context(nc.sbuf_tensor(name, shape, dt)).ap()
        b_smh = sbb("b_smh", [32, NR, TS], BF16)
        b_ops = sbb("b_ops", [128, 2, 512])
        b_rhs = sbb1("b_rhs", [128, 16, HT], BF16)
        b_w = sbb1("b_w", [128, 2, 16, 128], BF16)
        b_xz = sbb1("b_xz", [128, 8, HT + 3], BF16)
        b_halo = sbb1("b_halo", [128, 8, 3], BF16)
        b_zst = sbb1("b_zst", [128, 1, HT], BF16)
        b_xcm = sbb1("b_xcm", [128, 8, HT], BF16)
        b_ssm = sbb1("b_ssm", [33, TS])
        b_dtrb = sbb1("b_dtrb", [128, HT])
        b_dt = sbb1("b_dt", [128, 2, HT])
        b_dth = sbb1("b_dth", [128, 1, HT], BF16)
        b_sps = sbb1("b_sps", [128, 2, HT])

        sp_dt_v = dr["sp_dt"].rearrange("k p t -> p k t")
        sp_xc_v = dr["sp_xc"].rearrange("k p t -> p k t")
        sp_sz_v = dr["sp_sz"].rearrange("k p t -> p k t")

        def prescan(j):
            agx_dst_v = dr[f"agx_dst{j}"].rearrange(
                "(r kc p) t -> p kc r t", p=128, kc=16)
            for half in range(4):
                t0 = half * HT
                rb = B("brhs")
                s.dma("sp", "brhs", lambda h, half=half, v=agx_dst_v: h.dma_start(
                    out=b_rhs, in_=v[:, :, half, :]),
                    reads=[B(f"agxd{j}")], writes=[rb])
                if j == 0 and half == 0:
                    s.op("dve", lambda h: h.memset(b_halo, 0.0), writes=[B("bhalo")])
                s.op("dve", lambda h: h.tensor_copy(b_xz[:, :, 0:3], b_halo),
                     reads=[B("bhalo")], writes=[B("bxz")])
                for m in range(16):
                    par = m % 2
                    wb = B(f"bw{par}")
                    s.dma("sp", f"bw{par}", lambda h, m=m, par=par: h.dma_start(
                        out=b_w[:, par, :, :],
                        in_=dr["win_t"][m].rearrange("p kc mm -> p kc mm")),
                        reads=[B("win_spill")], writes=[wb])
                    pbank, pb = ((psA, B("psA")), (psB, B("psB")))[par]
                    pairs = [(b_w[:, par, kc, :], b_rhs[:, kc, :]) for kc in range(16)]
                    mmgroup(pbank, pairs, pb, [wb, rb])
                    if m < 8:
                        s.op("act", lambda h, m=m, pbank=pbank: h.activation(
                            b_xz[:, m, 3:3 + HT], pbank, AF.Copy),
                            reads=[pb], writes=[B("bxz")])
                    else:
                        s.op("act", lambda h, pbank=pbank: h.activation(
                            b_zst[:, 0, :], pbank, AF.Silu), reads=[pb], writes=[B("bzst")])
                        st("act", "spsz",
                           sp_sz_v[:, m - 8, j * TS + t0:j * TS + t0 + HT],
                           b_zst[:, 0, :], B("bzst"), wbufs=[B("spsz")])
                for k in range(8):
                    s.op("dve", lambda h, k=k: h.tensor_scalar(
                        b_xcm[:, k, :], b_xz[:, k, 0:HT], sb_convw[:, k, 0:1],
                        sb_convb[:, k:k + 1], ALU.mult, ALU.add),
                        reads=[B("bxz"), B("convw"), B("convb")], writes=[B("bxcm")])
                    for tap in range(1, 4):
                        s.op("dve", lambda h, k=k, tap=tap: h.scalar_tensor_tensor(
                            b_xcm[:, k, :], b_xz[:, k, tap:tap + HT],
                            sb_convw[:, k, tap:tap + 1], b_xcm[:, k, :],
                            ALU.mult, ALU.add),
                            reads=[B("bxz"), B("convw"), B("bxcm")], writes=[B("bxcm")])
                    s.op("act", lambda h, k=k: h.activation(
                        b_xcm[:, k, :], b_xcm[:, k, :], AF.Silu),
                        reads=[B("bxcm")], writes=[B("bxcm")])
                s.op("dve", lambda h: h.tensor_copy(b_halo, b_xz[:, :, HT:HT + 3]),
                     reads=[B("bxz")], writes=[B("bhalo")])
                st("act", "spxc", sp_xc_v[:, :, j * TS + t0:j * TS + t0 + HT],
                   b_xcm, B("bxcm"), wbufs=[B("spxc")])
                pairs = [(sb_xpj[:, kc * 33:(kc + 1) * 33], b_xcm[:, kc, :])
                         for kc in range(8)]
                mmgroup(psC[0:33, :], pairs, B("psC"), [B("xpj"), B("bxcm")])
                s.op("act", lambda h, t0=t0: h.activation(
                    b_ssm[0:33, t0:t0 + HT], psC[0:33, :], AF.Copy),
                    reads=[B("psC")], writes=[B("bssm")])
            st("act", f"ars{j}", dr[f"ars_src{j}"], b_ssm, B("bssm"), wbufs=[B(f"arss{j}")])
            s.cc(f"ccr{j}", lambda h, j=j: h.collective_compute(
                "AllReduce", ALU.add, replica_groups=G4,
                ins=[dr[f"ars_src{j}"]], outs=[dr[f"ars_dst{j}"]]),
                reads=[B(f"arss{j}")], writes=[B(f"arsd{j}")])
            ld("sp", "arsl", b_ssm, dr[f"ars_dst{j}"], B("bssm"), rbufs=[B(f"arsd{j}")])
            s.op("act", lambda h, j=j: h.activation(b_smh[:, j, :], b_ssm[0:32, :],
                                                    AF.Copy),
                 reads=[B("bssm")], writes=[B("bsmh")])
            for half in range(4):
                t0 = half * HT
                pbank, pb = ((psE, B("psE")), (psF, B("psF")))[half % 2]
                s.op("pe", lambda h, t0=t0, pbank=pbank: h.matmul(
                    pbank, sb_seldt, b_ssm[:, t0:t0 + HT], start=True, stop=True),
                    reads=[B("bssm"), B("seldt")], writes=[pb])
                s.op("dve", lambda h, pbank=pbank: h.tensor_copy(b_dtrb, pbank),
                     reads=[pb], writes=[B("bdtrb")])
                for k in range(8):
                    par = k % 2
                    s.op("dve", lambda h, k=k, par=par: h.tensor_scalar(
                        b_dt[:, par, :], b_dtrb, sb_dtw[:, k:k + 1], sb_dtb[:, k:k + 1],
                        ALU.mult, ALU.add),
                        reads=[B("bdtrb"), B("dtw"), B("dtb")], writes=[B(f"bdt{par}")])
                    s.op("act", lambda h, par=par: h.activation(
                        b_sps[:, par, :], b_dt[:, par, :], AF.Abs),
                        reads=[B(f"bdt{par}")], writes=[B(f"bsps{par}")])
                    s.op("act", lambda h, par=par: h.activation(
                        b_sps[:, par, :], b_sps[:, par, :], AF.Exp, scale=-1.0),
                        reads=[B(f"bsps{par}")], writes=[B(f"bsps{par}")])
                    s.op("act", lambda h, par=par: h.activation(
                        b_sps[:, par, :], b_sps[:, par, :], AF.Ln,
                        bias=sb_consts[:, C_ONE:C_ONE + 1]),
                        reads=[B(f"bsps{par}"), B("consts")], writes=[B(f"bsps{par}")])
                    s.op("dve", lambda h, par=par: h.scalar_tensor_tensor(
                        b_dt[:, par, :], b_dt[:, par, :], 0.0, b_sps[:, par, :],
                        ALU.max, ALU.add),
                        reads=[B(f"bdt{par}"), B(f"bsps{par}")], writes=[B(f"bdt{par}")])
                    s.op("act", lambda h, par=par: h.activation(
                        b_dth[:, 0, :], b_dt[:, par, :], AF.Copy),
                        reads=[B(f"bdt{par}")], writes=[B("bdth")])
                    st("act", "spdt",
                       sp_dt_v[:, k, j * TS + t0:j * TS + t0 + HT],
                       b_dth[:, 0, :], B("bdth"), wbufs=[B("spdt")])

        SCAN = {}
        def alloc_scan():
            SCAN["sc_dt"] = sbb("sc_dt", [128, 2, 8, TSC], BF16)
            SCAN["sc_xcz"] = sbb("sc_xcz", [128, 2, 8, TSC], BF16)
            SCAN["sc_dtu"] = sbb("sc_dtu", [128, 8, TSC], BF16)
            SCAN["sc_bc"] = sbb("sc_bc", [128, 2, 2, TSC], BF16)
            SCAN["sc_da"] = sbb("sc_da", [128, 2, TSC], BF16)
            SCAN["sc_bg"] = sbb("sc_bg", [128, 8, TSC], BF16)
            SCAN["sc_g"] = sbb("sc_g", [128, 8, TSC], BF16)
            SCAN["sc_h"] = sbb("sc_h", [128, 2, 8, TSC], BF16)
            SCAN["sc_y"] = sbb("sc_y", [128, 8, TSC], BF16)
            SCAN["ow_w"] = sbb("ow_w", [128, 2, 8, 128], BF16)

        def scanphase(j):
            sc_dt = SCAN["sc_dt"]; sc_xcz = SCAN["sc_xcz"]; sc_dtu = SCAN["sc_dtu"]
            sc_bc = SCAN["sc_bc"]; sc_da = SCAN["sc_da"]; sc_bg = SCAN["sc_bg"]
            sc_g = SCAN["sc_g"]; sc_h = SCAN["sc_h"]; sc_y = SCAN["sc_y"]
            ow_w = SCAN["ow_w"]
            rs_v = dr[f"rs_src{j}"].rearrange("(blk kc p) t -> p blk kc t",
                                              p=128, blk=4)
            for c in range(TS // TSC):
                tc = j * TS + c * TSC
                tcl = c * TSC
                cp = (j * (TS // TSC) + c) % 2
                dtb = B(f"scdt{cp}")
                xzb = B(f"scxcz{cp}")
                ld("sp", f"scdt{cp}", sc_dt[:, cp], sp_dt_v[:, :, tc:tc + TSC], dtb,
                   rbufs=[B("spdt")])
                ld("sp", f"scxc{cp}", sc_xcz[:, cp], sp_xc_v[:, :, tc:tc + TSC], xzb,
                   rbufs=[B("spxc")])
                s.op("dve", lambda h, cp=cp: h.tensor_tensor(
                    sc_dtu, sc_dt[:, cp], sc_xcz[:, cp], ALU.mult),
                    reads=[dtb, xzb], writes=[B("scdtu")])
                dbc = sb_D.unsqueeze(2).broadcast_to([128, 8, TSC])
                s.op("dve", lambda h, cp=cp, dbc=dbc: h.tensor_tensor(
                    sc_y, sc_xcz[:, cp], dbc, ALU.mult),
                    reads=[xzb, B("D")], writes=[B("scy")])
                ld("sp", f"scxc{cp}", sc_xcz[:, cp], sp_sz_v[:, :, tc:tc + TSC], xzb,
                   rbufs=[B("spsz")])
                for si in range(S):
                    par = si % 2
                    for row, base in ((0, si), (1, 16 + si)):
                        pbank, pb = ((psE, B("psE")), (psF, B("psF")))[row]
                        for q in range(2):
                            s.op("pe", lambda h, base=base, j=j, tcl=tcl, q=q, pbank=pbank:
                                 h.matmul(pbank, sb_sel[:, base, :],
                                          b_smh[:, j, tcl + q * 512:tcl + (q + 1) * 512],
                                          start=True, stop=True),
                                 reads=[B("bsmh"), B("sel")], writes=[pb])
                            s.op("act", lambda h, row=row, par=par, q=q, pbank=pbank:
                                 h.activation(sc_bc[:, par, row, q * 512:(q + 1) * 512],
                                              pbank, AF.Copy),
                                 reads=[pb], writes=[B(f"scbc{par}_{row}")])
                    bb = sc_bc[:, par, 0, :].unsqueeze(1).broadcast_to([128, 8, TSC])
                    cb = sc_bc[:, par, 1, :].unsqueeze(1).broadcast_to([128, 8, TSC])
                    s.op("dve", lambda h, bb=bb: h.tensor_tensor(sc_bg, sc_dtu, bb,
                                                                 ALU.mult),
                         reads=[B("scdtu"), B(f"scbc{par}_0")], writes=[B("scbg")])
                    for k in range(8):
                        kp = k % 2
                        s.op("act", lambda h, k=k, si=si, kp=kp, cp=cp: h.activation(
                            sc_da[:, kp, :], sc_dt[:, cp, k, :], AF.Exp,
                            scale=sb_A[:, k, si:si + 1]),
                            reads=[dtb, B("A")], writes=[B(f"scda{kp}")])
                        first = (j == 0 and c == 0)
                        carry_b = B(f"carry{k}_{si}")
                        s.op("dve", lambda h, k=k, si=si, kp=kp, par=par, first=first:
                             h.tensor_tensor_scan(
                                 sc_h[:, par, k, :], sc_da[:, kp, :], sc_bg[:, k, :],
                                 0.0 if first else sb_carry[:, k, si:si + 1],
                                 ALU.mult, ALU.add),
                             reads=[B(f"scda{kp}"), B("scbg")] +
                                   ([] if first else [carry_b]),
                             writes=[B(f"sch{par}")])
                        s.op("pool", lambda h, k=k, si=si, par=par: h.tensor_copy(
                            sb_carry[:, k, si:si + 1], sc_h[:, par, k, TSC - 1:TSC]),
                            reads=[B(f"sch{par}")], writes=[carry_b])
                    s.op("dve", lambda h, cb=cb, par=par: h.tensor_tensor(
                        sc_g, sc_h[:, par], cb, ALU.mult),
                        reads=[B(f"sch{par}"), B(f"scbc{par}_1")], writes=[B("scg")])
                    s.op("dve", lambda h: h.tensor_tensor(sc_y, sc_y, sc_g, ALU.add),
                         reads=[B("scg"), B("scy")], writes=[B("scy")])
                s.op("dve", lambda h, cp=cp: h.tensor_tensor(
                    sc_y, sc_y, sc_xcz[:, cp], ALU.mult),
                    reads=[B("scy"), xzb], writes=[B("scy")])
                for bcol in range(2):
                    blk = c * 2 + bcol
                    for m in range(16):
                        par = m % 2
                        owb = B(f"oww{par}")
                        s.dma("sp", f"oww{par}", lambda h, m=m, par=par: h.dma_start(
                            out=ow_w[:, par, :, :], in_=dr["wout_t"][m]),
                            reads=[B("wout_spill")], writes=[owb])
                        pbank, pb = ((psA, B("psA")), (psB, B("psB")))[par]
                        pairs = [(ow_w[:, par, kc, :],
                                  sc_y[:, kc, bcol * 512:(bcol + 1) * 512])
                                 for kc in range(8)]
                        mmgroup(pbank, pairs, pb, [owb, B("scy")])
                        s.op("act", lambda h, par=par, pbank=pbank: h.activation(
                            b_ops[:, par, :], pbank, AF.Copy),
                            reads=[pb], writes=[B(f"opst{par}")])
                        s.dma("act", f"rs{j}", lambda h, m=m, blk=blk, par=par,
                              v=rs_v: h.dma_start(
                            out=v[:, blk, m, :], in_=b_ops[:, par, :]),
                            reads=[B(f"opst{par}")], writes=[B(f"rss{j}")])
            s.cc(f"ccy{j}", lambda h, j=j: h.collective_compute(
                "ReduceScatter", ALU.add, replica_groups=G4,
                ins=[dr[f"rs_src{j}"]], outs=[dr[f"rs_dst{j}"]]),
                reads=[B(f"rss{j}")], writes=[B(f"rsd{j}")])

        if "B" in PHASES:
            prescan(0)
            prescan(1)
            prescan(2)
            prescan(3)
            emit_quant()
            s.barrier()
            es_b1.close()
            alloc_scan()
            scanphase(0)
            scanphase(1)
            scanphase(2)
            scanphase(3)

        s.barrier()
        es_b.close()
        es_c = ExitStack()
        def sbc(name, shape, dt=F32):
            return es_c.enter_context(nc.sbuf_tensor(name, shape, dt)).ap()
        c_in = sbc("c_in", [128, 16, BLK])
        c_tok = sbc("c_tok", [128, 4, D])
        c_x = sbc("c_x", [128, D])
        c_ln2 = sbc("c_ln2", [128, D])
        c_xn2 = sbc("c_xn2", [128, D])
        c_scr = sbc("c_scr", [128, D], BF16)
        c_xq = sbc("c_xq", [128, D], BF16)
        c_xqT = sbc("c_xqT", [128, 16, 128], BF16)
        c_stat = sbc("c_stat", [128, 12])
        c_sxr = sbc("c_sxr", [1, BLK])
        sb_fnw = sbc("sb_fnw", [128, D])
        ld("sp", "ldfnw", sb_fnw, fnorm_bc, B("fnw"))

        def ln_stats(tile_ap, tilebuf, eps_col):
            s.op("act", lambda h: h.activation(c_scr, tile_ap, AF.Copy,
                                               accum_out=c_stat[:, 0:1]),
                 reads=[tilebuf], writes=[B("cscr"), B("cst")])
            s.op("act", lambda h: h.activation(c_scr, tile_ap, AF.Square,
                                               accum_out=c_stat[:, 1:2]),
                 reads=[tilebuf], writes=[B("cscr"), B("cst")])
            s.op("dve", lambda h: h.tensor_scalar(c_stat[:, 2:3], c_stat[:, 0:1],
                                                  1.0 / D, None, ALU.mult),
                 reads=[B("cst")], writes=[B("cst")])
            s.op("dve", lambda h: h.tensor_scalar(c_stat[:, 3:4], c_stat[:, 1:2],
                                                  1.0 / D, None, ALU.mult),
                 reads=[B("cst")], writes=[B("cst")])
            s.op("dve", lambda h: h.tensor_scalar(c_stat[:, 4:5], c_stat[:, 2:3],
                                                  c_stat[:, 2:3], None, ALU.mult),
                 reads=[B("cst")], writes=[B("cst")])
            s.op("dve", lambda h: h.tensor_tensor(c_stat[:, 3:4], c_stat[:, 3:4],
                                                  c_stat[:, 4:5], ALU.subtract),
                 reads=[B("cst")], writes=[B("cst")])
            s.op("act", lambda h, eps_col=eps_col: h.activation(
                c_stat[:, 5:6], c_stat[:, 3:4], AF.Ln,
                bias=sb_consts[:, eps_col:eps_col + 1]),
                 reads=[B("cst"), B("consts")], writes=[B("cst")])
            s.op("act", lambda h: h.activation(c_stat[:, 5:6], c_stat[:, 5:6], AF.Exp,
                                               scale=-0.5),
                 reads=[B("cst")], writes=[B("cst")])
            s.op("dve", lambda h: h.tensor_scalar(c_stat[:, 8:9], c_stat[:, 2:3],
                                                  c_stat[:, 5:6], -1.0, ALU.mult, ALU.mult),
                 reads=[B("cst")], writes=[B("cst")])

        for j in range(NR):
            rsd_v = dr[f"rs_dst{j}"].rearrange("(kc p) t -> p kc t", p=128)
            ld("sp", "cin", c_in, rsd_v, B("cin"), rbufs=[B(f"rsd{j}")])
            for kc in range(16):
                for tt in range(4):
                    par = (kc * 4 + tt) % 2
                    pbank, pb = ((psE, B("psE")), (psF, B("psF")))[par]
                    s.op("pe", lambda h, kc=kc, tt=tt, pbank=pbank: h.transpose(
                        pbank[:, 0:128], c_in[:, kc, tt * 128:(tt + 1) * 128], sb_ident_f),
                        reads=[B("cin"), B("identf")], writes=[pb])
                    s.op("act", lambda h, kc=kc, tt=tt, pbank=pbank: h.activation(
                        c_tok[:, tt, kc * 128:(kc + 1) * 128], pbank[:, 0:128], AF.Copy),
                        reads=[pb], writes=[B("ctok")])
            x1_v = dr[f"x1_sp{j}"].rearrange("(tt p) d -> p tt d", p=128)
            ln2_v = dr[f"ln2_sp{j}"].rearrange("(tt p) d -> p tt d", p=128)
            xqT_v = dr[f"xqT_sp{j}"].rearrange("(kc p) t -> p kc t", p=128)
            for tt in range(4):
                ld("sp", "cx", c_x, x_own_v[:, j, tt, :], B("cx"))
                ln_stats(c_x, B("cx"), C_EPS5)
                s.op("act", lambda h: h.activation(
                    c_xn2, c_x, AF.Identity,
                    scale=c_stat[:, 5:6], bias=c_stat[:, 8:9]),
                    reads=[B("cx"), B("cst")], writes=[B("cxn2")])
                s.op("dve", lambda h, tt=tt: h.tensor_tensor(
                    c_tok[:, tt, :], c_tok[:, tt, :], c_xn2, ALU.add),
                    reads=[B("ctok"), B("cxn2")], writes=[B("ctok")])
                ln_stats(c_tok[:, tt, :], B("ctok"), C_EPS5)
                s.op("act", lambda h, tt=tt: h.activation(
                    c_xn2, c_tok[:, tt, :], AF.Identity,
                    scale=c_stat[:, 5:6], bias=c_stat[:, 8:9]),
                    reads=[B("ctok"), B("cst")], writes=[B("cxn2")])
                s.op("dve", lambda h: h.tensor_tensor(c_x, c_x, c_xn2, ALU.add),
                     reads=[B("cx"), B("cxn2")], writes=[B("cx")])
                st("act", "cx1", x1_v[:, tt, :], c_x, B("cx"), wbufs=[B(f"x1s{j}")])
                ln_stats(c_x, B("cx"), C_EPS5)
                s.op("act", lambda h: h.activation(
                    c_ln2, c_x, AF.Identity,
                    scale=c_stat[:, 5:6], bias=c_stat[:, 8:9]),
                    reads=[B("cx"), B("cst")], writes=[B("cln2")])
                st("act", "cln2", ln2_v[:, tt, :], c_ln2, B("cln2"), wbufs=[B(f"ln2s{j}")])
                s.op("act", lambda h: h.activation(c_stat[:, 6:7], c_stat[:, 5:6],
                                                   AF.Square),
                     reads=[B("cst")], writes=[B("cst")])
                s.op("dve", lambda h: h.tensor_tensor(c_stat[:, 6:7], c_stat[:, 6:7],
                                                      c_stat[:, 3:4], ALU.mult),
                     reads=[B("cst")], writes=[B("cst")])
                s.op("act", lambda h: h.activation(
                    c_stat[:, 6:7], c_stat[:, 6:7], AF.Ln,
                    bias=sb_consts[:, C_EPS6:C_EPS6 + 1]),
                    reads=[B("cst"), B("consts")], writes=[B("cst")])
                s.op("act", lambda h: h.activation(c_stat[:, 6:7], c_stat[:, 6:7], AF.Exp,
                                                   scale=-0.5),
                     reads=[B("cst")], writes=[B("cst")])
                s.op("act", lambda h: h.activation(
                    c_xn2, c_ln2, AF.Identity, scale=c_stat[:, 6:7]),
                    reads=[B("cln2"), B("cst")], writes=[B("cxn2")])
                s.op("dve", lambda h: h.tensor_tensor(c_xn2, c_xn2, sb_fnw, ALU.mult),
                     reads=[B("cxn2"), B("fnw")], writes=[B("cxn2")])
                s.op("dve", lambda h: h.tensor_reduce(
                    c_stat[:, 9:10], c_xn2, mybir.AxisListType.X, ALU.max,
                    apply_absolute_value=True),
                    reads=[B("cxn2")], writes=[B("cst")])
                s.op("dve", lambda h: h.tensor_scalar(c_stat[:, 9:10], c_stat[:, 9:10],
                                                      1e-8, None, ALU.max),
                     reads=[B("cst")], writes=[B("cst")])
                s.op("dve", lambda h: h.reciprocal(c_stat[:, 10:11], c_stat[:, 9:10]),
                     reads=[B("cst")], writes=[B("cst")])
                s.op("dve", lambda h: h.tensor_scalar(c_stat[:, 11:12], c_stat[:, 9:10],
                                                      1.0 / 127.0, None, ALU.mult),
                     reads=[B("cst")], writes=[B("cst")])
                s.op("dve", lambda h: h.tensor_scalar(c_stat[:, 10:11], c_stat[:, 10:11],
                                                      127.0, None, ALU.mult),
                     reads=[B("cst")], writes=[B("cst")])
                s.op("dve", lambda h: h.tensor_scalar(c_xn2, c_xn2,
                                                      c_stat[:, 10:11], None, ALU.mult),
                     reads=[B("cxn2"), B("cst")], writes=[B("cxn2")])
                s.op("dve", lambda h: h.tensor_scalar(c_xn2, c_xn2, MAGIC, MAGIC,
                                                      ALU.add, ALU.subtract),
                     reads=[B("cxn2")], writes=[B("cxn2")])
                s.op("dve", lambda h: h.tensor_scalar(c_xq, c_xn2, 127.0, -127.0,
                                                      ALU.min, ALU.max),
                     reads=[B("cxn2")], writes=[B("cxq")])
                for kc in range(16):
                    pg = (psG, psH)[kc % 2]
                    pgb = B(f"psT{kc % 2}")
                    s.op("pe", lambda h, kc=kc, pg=pg: h.transpose(
                        pg[:, 0:128], c_xq[:, kc * 128:(kc + 1) * 128], sb_ident_h),
                        reads=[B("cxq"), B("identh")], writes=[pgb])
                    s.op("act", lambda h, kc=kc, pg=pg: h.activation(
                        c_xqT[:, kc, :], pg[:, 0:128], AF.Copy),
                        reads=[pgb], writes=[B("cxqT")])
                st("act", "cxq", xqT_v[:, :, tt * 128:(tt + 1) * 128], c_xqT,
                   B("cxqT"), wbufs=[B(f"xqs{j}")])
                s.op("pe", lambda h: h.transpose(
                    psE[0:1, 0:128], c_stat[:, 11:12], sb_ident_f),
                    reads=[B("cst"), B("identf")], writes=[B("psE")])
                s.op("act", lambda h, tt=tt: h.activation(
                    c_sxr[0:1, tt * 128:(tt + 1) * 128], psE[0:1, 0:128], AF.Copy),
                    reads=[B("psE")], writes=[B("csxr")])
            st("act", "csx", dr[f"sx_sp{j}"], c_sxr, B("csxr"), wbufs=[B(f"sxs{j}")])
        if DEV_DBG:
            s.dma("sp", "dbg", lambda h: h.dma_start(
                out=dbg["dbg_x1"], in_=dr["x1_sp0"]),
                reads=[B("x1s0")], writes=[B("dbgx1")])
            s.dma("sp", "dbg", lambda h: h.dma_start(
                out=dbg["dbg_rs"], in_=dr["rs_dst0"]),
                reads=[B("rsd0")], writes=[B("dbgrs")])
            s.dma("sp", "dbg", lambda h: h.dma_start(
                out=dbg["dbg_xq"].rearrange("(kc p) t -> p kc t", p=128),
                in_=dr["xqT_sp0"].rearrange("(kc p) t -> p kc t", p=128)),
                reads=[B("xqs0")], writes=[B("dbgxq")])

        s.barrier()
        es_c.close()
        es_f = ExitStack()
        def sbf(name, shape, dt=F32):
            return es_f.enter_context(nc.sbuf_tensor(name, shape, dt)).ap()
        f_h = sbf("f_h", [128, 64, BLK], BF16)
        f_xq = sbf("f_xq", [128, 16, BLK], BF16)
        f_w = sbf("f_w", [128, 2, 16, 128], BF16)
        f_wd = sbf("f_wd", [128, 2, 64, 128], BF16)
        f_sxs = sbf("f_sxs", [128, BLK])
        f_scr = sbf("f_scr", [128, BLK])
        f_dn = sbf("f_dn", [128, 2, BLK])
        f_tok = sbf("f_tok", [128, 4, D])
        f_ln = sbf("f_ln", [128, D])
        f_res = sbf("f_res", [128, D])
        f_sscr = sbf("f_sscr", [128, D], BF16)
        f_stat = sbf("f_stat", [128, 12])
        f_srow = sbf("f_srow", [1, BLK])
        sb_sdg = sbf("sb_sdg", [128, 16])
        s.op("dve", lambda h: h.tensor_scalar(sb_sdg, sb_fg, sb_rcol[:, 1:2],
                                              None, ALU.mult),
             reads=[B("fg"), B("rcol")], writes=[B("sdg")])

        def f_ln_stats(tile_ap, tilebuf, eps_col):
            s.op("act", lambda h: h.activation(f_sscr, tile_ap, AF.Copy,
                                               accum_out=f_stat[:, 0:1]),
                 reads=[tilebuf], writes=[B("fscr2"), B("fst")])
            s.op("act", lambda h: h.activation(f_sscr, tile_ap, AF.Square,
                                               accum_out=f_stat[:, 1:2]),
                 reads=[tilebuf], writes=[B("fscr2"), B("fst")])
            s.op("dve", lambda h: h.tensor_scalar(f_stat[:, 2:3], f_stat[:, 0:1],
                                                  1.0 / D, None, ALU.mult),
                 reads=[B("fst")], writes=[B("fst")])
            s.op("dve", lambda h: h.tensor_scalar(f_stat[:, 3:4], f_stat[:, 1:2],
                                                  1.0 / D, None, ALU.mult),
                 reads=[B("fst")], writes=[B("fst")])
            s.op("dve", lambda h: h.tensor_scalar(f_stat[:, 4:5], f_stat[:, 2:3],
                                                  f_stat[:, 2:3], None, ALU.mult),
                 reads=[B("fst")], writes=[B("fst")])
            s.op("dve", lambda h: h.tensor_tensor(f_stat[:, 3:4], f_stat[:, 3:4],
                                                  f_stat[:, 4:5], ALU.subtract),
                 reads=[B("fst")], writes=[B("fst")])
            s.op("act", lambda h, eps_col=eps_col: h.activation(
                f_stat[:, 5:6], f_stat[:, 3:4], AF.Ln,
                bias=sb_consts[:, eps_col:eps_col + 1]),
                 reads=[B("fst"), B("consts")], writes=[B("fst")])
            s.op("act", lambda h: h.activation(f_stat[:, 5:6], f_stat[:, 5:6], AF.Exp,
                                               scale=-0.5),
                 reads=[B("fst")], writes=[B("fst")])
            s.op("dve", lambda h: h.tensor_scalar(f_stat[:, 8:9], f_stat[:, 2:3],
                                                  f_stat[:, 5:6], -1.0, ALU.mult, ALU.mult),
                 reads=[B("fst")], writes=[B("fst")])

        out_v = out_own.rearrange("(j tt p) d -> p j tt d", p=128, j=NR)
        for j in range(NR):
            xqT_v = dr[f"xqT_sp{j}"].rearrange("(kc p) t -> p kc t", p=128)
            ld("sp", "fxq", f_xq, xqT_v, B("fxq"), rbufs=[B(f"xqs{j}")])
            ld("sp", "fsx", f_srow, dr[f"sx_sp{j}"], B("fsrow"), rbufs=[B(f"sxs{j}")])
            s.op("dve", lambda h: h.tensor_scalar(f_srow, f_srow,
                                                  sb_scales[0:1, 0:1], None, ALU.mult),
                 reads=[B("fsrow"), B("scales")], writes=[B("fsrow")])
            s.op("pe", lambda h: h.matmul(psE, sb_ones, f_srow, start=True, stop=True),
                 reads=[B("fsrow"), B("ones")], writes=[B("psE")])
            s.op("dve", lambda h: h.tensor_copy(f_sxs, psE),
                 reads=[B("psE")], writes=[B("fsxs")])
            for mh in range(64):
                par = mh % 2
                srcw = dr["wqu_dst"][mh]
                fwb = B(f"fw{par}")
                s.dma("sp", f"fw{par}", lambda h, par=par, srcw=srcw: h.dma_start(
                    out=f_w[:, par, :, :], in_=srcw),
                    reads=[B("wqu")], writes=[fwb])
                pbank, pb = ((psC, B("psC")), (psD, B("psD")))[par]
                pairs = [(f_w[:, par, kc, :], f_xq[:, kc, :]) for kc in range(16)]
                mmgroup(pbank, pairs, pb, [fwb, B("fxq")])
                s.op("dve", lambda h, par=par, pbank=pbank: h.tensor_tensor(
                    f_scr, pbank, f_sxs, ALU.mult),
                    reads=[pb, B("fsxs")], writes=[B("fscr")])
                s.op("act", lambda h, mh=mh: h.activation(
                    f_h[:, mh, :], f_scr, AF.Silu),
                    reads=[B("fscr")], writes=[B("fh")])
            for dt16 in range(16):
                par = dt16 % 2
                srcd = dr["wqd_dst"][dt16]
                fdb = B(f"fwd{par}")
                s.dma("sp", f"fwd{par}", lambda h, par=par, srcd=srcd: h.dma_start(
                    out=f_wd[:, par, :, :], in_=srcd),
                    reads=[B("wqd")], writes=[fdb])
                pbank, pb = ((psA, B("psA")), (psB, B("psB")))[par]
                pairs = [(f_wd[:, par, kh, :], f_h[:, kh, :]) for kh in range(64)]
                mmgroup(pbank, pairs, pb, [fdb, B("fh")])
                s.op("dve", lambda h, par=par, dt16=dt16, pbank=pbank: h.tensor_scalar(
                    f_dn[:, par, :], pbank, sb_sdg[:, dt16:dt16 + 1], None, ALU.mult),
                    reads=[pb, B("sdg")], writes=[B(f"fdn{par}")])
                for tt in range(4):
                    tpar = tt % 2
                    pe_b, peb = ((psE, B("psE")), (psF, B("psF")))[tpar]
                    s.op("pe", lambda h, par=par, tt=tt, pe_b=pe_b: h.transpose(
                        pe_b[:, 0:128], f_dn[:, par, tt * 128:(tt + 1) * 128], sb_ident_f),
                        reads=[B(f"fdn{par}"), B("identf")], writes=[peb])
                    s.op("act", lambda h, tt=tt, dt16=dt16, pe_b=pe_b: h.activation(
                        f_tok[:, tt, dt16 * 128:(dt16 + 1) * 128], pe_b[:, 0:128],
                        AF.Copy), reads=[peb], writes=[B("ftok")])
            ln2_v = dr[f"ln2_sp{j}"].rearrange("(tt p) d -> p tt d", p=128)
            x1_v = dr[f"x1_sp{j}"].rearrange("(tt p) d -> p tt d", p=128)
            for tt in range(4):
                ld("sp", "fres", f_res, ln2_v[:, tt, :], B("fres"), rbufs=[B(f"ln2s{j}")])
                s.op("dve", lambda h, tt=tt: h.tensor_tensor(
                    f_tok[:, tt, :], f_tok[:, tt, :], f_res, ALU.add),
                    reads=[B("ftok"), B("fres")], writes=[B("ftok")])
                f_ln_stats(f_tok[:, tt, :], B("ftok"), C_EPS5)
                s.op("act", lambda h, tt=tt: h.activation(
                    f_ln, f_tok[:, tt, :], AF.Identity,
                    scale=f_stat[:, 5:6], bias=f_stat[:, 8:9]),
                    reads=[B("ftok"), B("fst")], writes=[B("fln")])
                ld("sp", "fres", f_res, x1_v[:, tt, :], B("fres"), rbufs=[B(f"x1s{j}")])
                s.op("dve", lambda h: h.tensor_tensor(f_res, f_res, f_ln, ALU.add),
                     reads=[B("fres"), B("fln")], writes=[B("fres")])
                st("act", "fout", out_v[:, j, tt, :], f_res, B("fres"),
                   wbufs=[B("outw")])

        es_f.close()
        s.wait_all_done("act")
        s.wait_all_done("sp")
        s.build(block)
    return nc, list(I.keys()), dbg


def _host_inputs(inputs):
    x = np.asarray(inputs["L0_attn_x"])
    W_in = np.asarray(inputs["L0_attn_in_proj_w"])
    conv_w = np.asarray(inputs["L0_attn_conv_w"])
    conv_b = np.asarray(inputs["L0_attn_conv_b"])
    xproj = np.asarray(inputs["L0_attn_x_proj_w"])
    dt_w = np.asarray(inputs["L0_attn_dt_w"])
    dt_b = np.asarray(inputs["L0_attn_dt_b"])
    A_log = np.asarray(inputs["L0_attn_A_log"])
    D_param = np.asarray(inputs["L0_attn_D_param"])
    W_out = np.asarray(inputs["L0_attn_out_proj_w"])
    W_up = np.asarray(inputs["L0_ffn_weight_up"])
    W_dn = np.asarray(inputs["L0_ffn_weight_down"])
    anorm = np.asarray(inputs["L0_attn_norm_w"])
    fnorm = np.asarray(inputs["L0_ffn_norm_weight"])
    fgamma = np.asarray(inputs["L0_ffn_gamma"])

    import ml_dtypes
    ident_h = np.eye(128, dtype=ml_dtypes.bfloat16)
    ident_f = np.eye(128, dtype=np.float32)
    ones_row = np.ones((1, 128), np.float32)
    consts = np.zeros((128, C_NCONST), np.float32)
    consts[:, C_ONE] = 1.0
    consts[:, C_EPS5] = 1e-5
    consts[:, C_EPS6] = 1e-6
    consts[:, C_EPS8] = 1e-8

    sel_bc_h = np.zeros((32, 32, 128), ml_dtypes.bfloat16)
    for si in range(32):
        sel_bc_h[si, si, :] = 1.0
    sel_dt_h = np.zeros((33, 128), np.float32)
    sel_dt_h[32, :] = 1.0

    woutT = np.ascontiguousarray(W_out.T)
    wupT = np.ascontiguousarray(W_up.T)
    wdnT = np.ascontiguousarray(W_dn.T)

    maps = []
    for r in range(8):
        b, g = divmod(r, 4)
        xb = x[b]
        x_own_a = np.ascontiguousarray(np.concatenate(
            [xb[j * 2048 + g * 512: j * 2048 + (g + 1) * 512] for j in range(NR)], axis=0))
        winT_a = np.ascontiguousarray(np.concatenate(
            [W_in[g * 1024:(g + 1) * 1024], W_in[4096 + g * 1024:4096 + (g + 1) * 1024]],
            axis=0).T)
        maps.append({
            "x_own": x_own_a,
            "winT": winT_a,
            "woutT": np.ascontiguousarray(woutT[g * 1024:(g + 1) * 1024, :]),
            "xprojT": np.ascontiguousarray(xproj[:, g * 1024:(g + 1) * 1024].T),
            "conv_w": np.ascontiguousarray(
                conv_w[g * 1024:(g + 1) * 1024, 0, :].reshape(8, 128, 4).transpose(1, 0, 2)),
            "conv_b": np.ascontiguousarray(
                conv_b[g * 1024:(g + 1) * 1024].reshape(8, 128).T),
            "dt_w": np.ascontiguousarray(
                dt_w[g * 1024:(g + 1) * 1024, 0].reshape(8, 128).T),
            "dt_b": np.ascontiguousarray(
                dt_b[g * 1024:(g + 1) * 1024].reshape(8, 128).T),
            "a_log": np.ascontiguousarray(
                A_log[g * 1024:(g + 1) * 1024].reshape(8, 128, S).transpose(1, 0, 2)),
            "d_prm": np.ascontiguousarray(
                D_param[g * 1024:(g + 1) * 1024].reshape(8, 128).T),
            "anorm_w": np.ascontiguousarray(anorm.reshape(16, 128).T),
            "fnorm_bc": np.ascontiguousarray(np.tile(fnorm[None, :], (128, 1))),
            "fgam": np.ascontiguousarray(fgamma.reshape(16, 128).T),
            "wupT_sl": np.ascontiguousarray(wupT[:, r * 1024:(r + 1) * 1024]),
            "wdnT_sl": np.ascontiguousarray(wdnT[:, r * 256:(r + 1) * 256]),
            "ident_h": ident_h,
            "ident_f": ident_f,
            "ones_row": ones_row,
            "ones_col": np.ones((128, 1), np.float32),
            "sel_bc": sel_bc_h, "sel_dt": sel_dt_h,
            "consts": consts,
        })
    return maps


def kernel(**inputs):
    nc, in_names, dbg = build_program()
    maps = _host_inputs(inputs)
    res = run_bass_kernel_spmd(nc, maps, core_ids=list(range(8)))
    out = np.zeros((2, T, D), np.float32)
    for r in range(8):
        b, g = divmod(r, 4)
        if "out_own" in res.results[r]:
            oo = res.results[r]["out_own"]
            for j in range(NR):
                out[b, j * 2048 + g * 512: j * 2048 + (g + 1) * 512] = \
                    oo[j * BLK:(j + 1) * BLK]
    return out
